# version 07
# baseline (speedup 1.0000x reference)
import numpy as np
import ml_dtypes
import concourse.bass as bass
import concourse.mybir as mybir
from concourse.bass_utils import run_bass_kernel_spmd

BF16 = mybir.dt.bfloat16
F32 = mybir.dt.float32
AF = mybir.ActivationFunctionType
ALU = mybir.AluOpType
AX = mybir.AxisListType.X
BF = ml_dtypes.bfloat16

D, NDC, T, TH, NTH = 2048, 16, 1024, 512, 2
FS, FE, NE = 2816, 1408, 8
NFT_S, NFT_E = 22, 11
NH = 16
EPS = 1e-5
ISQ = float(1.0 / np.sqrt(128.0))
BIGNEG = -3.0e30
PADB = -100.0
WSLOT = 3
WSLOT_W = 2048

OFF_LNX = 0
OFF_HB = 16384
OFF_VSB = 16384
OFF_KST = 32768
OFF_QT = 34816
OFF_VB = 0
OFF_PB = 8192
OFF_Z = 10240
OFF_KB = 19456
OFF_LNSCR = 33280
OFF_NSCR = 33280
ARENA_N = 51200
KOFF = 0
VOFF = 2097152
KVN = 4194304
TOFF = 4194304
KVN2 = 4194320

ATTN_ON = True
FFN2_ON = True


def build_program():
    nc = bass.Bass("TRN2", target_bir_lowering=False)
    dt = nc.dram_tensor
    x_in = dt("x_in", [NDC, 128, T], F32, kind="ExternalInput").ap()
    wr_in = dt("wr_in", [128, NDC * 8], F32, kind="ExternalInput").ap()
    wsg_s = dt("wsg_s", [NFT_S, 128, 2048], F32, kind="ExternalInput").ap()
    wsu_s = dt("wsu_s", [NFT_S, 128, 2048], F32, kind="ExternalInput").ap()
    wg_s = dt("wg_s", [NE, NFT_E, 128, 2048], F32, kind="ExternalInput").ap()
    wu_s = dt("wu_s", [NE, NFT_E, 128, 2048], F32, kind="ExternalInput").ap()
    wsd_s = dt("wsd_s", [16, 128, FS], F32, kind="ExternalInput").ap()
    wd_s = dt("wd_s", [NE, 16, 128, FE], F32, kind="ExternalInput").ap()
    wk_s = dt("wk_s", [16, 128, 2048], F32, kind="ExternalInput").ap()
    wv_s = dt("wv_s", [4, 4, 128, 2048], F32, kind="ExternalInput").ap()
    wq_s = dt("wq_s", [16, 128, 2048], F32, kind="ExternalInput").ap()
    wo_s = dt("wo_s", [16, 128, 2048], F32, kind="ExternalInput").ap()
    m0_in = dt("m0_in", [128, 896], BF16, kind="ExternalInput").ap()
    identf_in = dt("identf_in", [128, 128], F32, kind="ExternalInput").ap()
    identb_in = dt("identb_in", [128, 128], BF16, kind="ExternalInput").ap()
    c2048_in = dt("c2048_in", [128, 1], BF16, kind="ExternalInput").ap()
    onescol_in = dt("onescol_in", [128, 1], BF16, kind="ExternalInput").ap()
    onesrow_in = dt("onesrow_in", [1, 128], F32, kind="ExternalInput").ap()
    negisq_in = dt("negisq_in", [1, 128], F32, kind="ExternalInput").ap()
    sumind_in = dt("sumind_in", [128, 10], BF16, kind="ExternalInput").ap()
    scaletab_in = dt("scaletab_in", [128, 10], F32, kind="ExternalInput").ap()
    padbias_in = dt("padbias_in", [128, 10], F32, kind="ExternalInput").ap()
    fconst_in = dt("fconst_in", [128, 3], F32, kind="ExternalInput").ap()
    y_out = dt("y_out", [NDC, 128, T], F32, kind="ExternalOutput").ap()
    k_send = dt("k_send", [2097168], BF16).ap()
    k_gath = dt("k_gath", [4, 2097168], BF16).ap()
    v_send = dt("v_send", [2097152], BF16).ap()
    v_gath = dt("v_gath", [4, 2097152], BF16).ap()

    from contextlib import ExitStack
    stack = ExitStack()
    sbt = lambda name, shape, dtyp: stack.enter_context(nc.sbuf_tensor(name, shape, dtyp))

    x = sbt("x", [128, NDC * T], F32)
    arena = sbt("arena", [128, ARENA_N], BF16)
    wring = sbt("wring", [128, WSLOT * WSLOT_W], BF16)
    sgscr = sbt("sgscr", [128, 4 * TH], F32)
    uwscr = sbt("uwscr", [128, TH], F32)
    wbscr = sbt("wbscr", [128, T], F32)
    rowpool = sbt("rowpool", [1, 2048], F32)
    negc = sbt("negc", [128, 2], F32)
    btab = sbt("btab", [128, 16], F32)
    wsb = sbt("wsb", [64, 128], F32)
    wtok = sbt("wtok", [128, 64], F32)
    rscr = sbt("rscr", [128, 48], F32)
    m0 = sbt("m0", [128, 896], BF16)
    identf = sbt("identf", [128, 128], F32)
    identb = sbt("identb", [128, 128], BF16)
    c2048 = sbt("c2048", [128, 1], BF16)
    onescol = sbt("onescol", [128, 1], BF16)
    onesrow = sbt("onesrow", [1, 128], F32)
    negisq = sbt("negisq", [1, 128], F32)
    sumind = sbt("sumind", [128, 10], BF16)
    scaletab = sbt("scaletab", [128, 10], F32)
    padbias = sbt("padbias", [128, 10], F32)
    wrb = sbt("wrb", [128, 128], BF16)
    fconst = sbt("fconst", [128, 3], F32)
    nc.const_aps.register(F32, EPS, fconst[:, 0:1])
    nc.const_aps.register(F32, ISQ, fconst[:, 1:2])
    nc.const_aps.register(F32, 1.21, fconst[:, 2:3])
    qn_sb = sbt("qn_sb", [1, 32], F32)
    kq_sb = sbt("kq_sb", [1, 16], BF16)
    knb_sb = sbt("knb_sb", [1, 64], BF16)
    kmx = sbt("kmx", [1, 2], F32)
    nscr = sbt("nscr", [128, 1024], BF16)
    cqs = sbt("cqs", [1, 2], F32)
    cscs = sbt("cscs", [1, 2], F32)
    crow = sbt("crow", [1, 128], F32)
    invsc = rowpool

    psum = stack.enter_context(nc.psum_tensor("psum", [128, 4096], F32))
    BANK = lambda k: psum[:, k * 512:(k + 1) * 512]

    class Sem:
        __slots__ = ("h", "n")

        def __init__(self, name):
            self.h = stack.enter_context(nc.semaphore(name))
            self.n = 0

        def bump(self, k=1):
            self.n += k
            return self.n

    nm = ("s_const s_x s_wfree s_cast s_stat s_lnrow s_lnb s_lnx s_rt s_rw s_rwt "
          "s_rwe s_rext s_wb s_wbe s_fft s_sg s_h s_hfree s_z s_zf "
          "s_kproj s_ke s_ksend s_vproj s_ve s_vsend s_qproj s_qe s_cc "
          "s_kfree s_vfree s_sa s_saf s_cb s_sb s_exp s_pf s_inv s_invb "
          "s_zev s_ofree s_o s_of s_out s_wrb s_lnsq s_kn1 s_kn2 s_knl s_qn1 s_qn2 s_cq s_csc").split()
    g = {n: Sem(n) for n in nm}
    s_wq = [Sem(f"s_wq{i}") for i in range(WSLOT)]
    s_kl = [Sem(f"s_kl{i}") for i in range(3)]
    s_ks = [Sem(f"s_ks{i}") for i in range(2)]
    s_vl = [Sem(f"s_vl{i}") for i in range(2)]

    PT, PV, PS, PY, PG = [], [], [], [], []

    bank_free = [None] * 8

    def bank_wait_t(k):
        w = []
        if bank_free[k] is not None:
            sm, n = bank_free[k]
            w.append((sm.h, n))
            bank_free[k] = None
        return w

    def set_bank(k, sm, n):
        bank_free[k] = (sm, n)

    wi = [0]
    ring_free = [None] * WSLOT

    def load_wchunk(src_ap, width):
        i = wi[0]
        wi[0] += 1
        slot = i % WSLOT
        sw = s_wq[slot]
        tgt = sw.bump(16)
        fw = ring_free[slot]
        ring_free[slot] = "PENDING"
        assert fw != "PENDING", f"chunk slot {slot} reloaded before free event set"

        def g_op(gp, fw=fw, slot=slot, src_ap=src_ap, width=width, sw=sw):
            if fw is not None:
                gp.wait_ge(fw[0], fw[1])
            gp.dma_start(out=wring[:, slot * WSLOT_W: slot * WSLOT_W + width],
                         in_=src_ap).then_inc(sw.h, 16)
        PG.append(g_op)
        return (slot, sw.h, tgt)

    def chunk_free_at(ch, sem, count):
        slot = ch[0]
        assert ring_free[slot] == "PENDING"
        ring_free[slot] = (sem.h, count)

    def wap(ch, a, b):
        slot, _, _ = ch
        return wring[:, slot * WSLOT_W + a: slot * WSLOT_W + b]

    def y_load(y):
        for (dst, src) in [(m0, m0_in), (identf, identf_in), (identb, identb_in),
                           (c2048, c2048_in), (onescol, onescol_in),
                           (onesrow, onesrow_in), (negisq, negisq_in),
                           (sumind, sumind_in), (scaletab, scaletab_in),
                           (padbias, padbias_in), (fconst, fconst_in)]:
            y.dma_start(out=dst[:, :], in_=src[:, :]).then_inc(g["s_const"].h, 16)
        for dc in range(NDC):
            y.dma_start(out=x[:, dc * T:(dc + 1) * T], in_=x_in[dc, :, :]).then_inc(g["s_x"].h, 16)
    PY.append(y_load)
    g["s_const"].bump(16 * 11)
    g["s_x"].bump(16 * NDC)

    def g_load(gp):
        gp.dma_start(out=wrb[:, :], in_=wr_in[:, :]).then_inc(g["s_wrb"].h, 16)
    PG.append(g_load)
    g["s_wrb"].bump(16)

    def ph_ln(gate_waits):
        ab = lambda sl: arena[:, OFF_LNSCR + sl * 1024: OFF_LNSCR + (sl + 1) * 1024]
        sq = lambda sl: arena[:, OFF_LNSCR + 2048 + sl * 1024: OFF_LNSCR + 2048 + (sl + 1) * 1024]
        cast_base = g["s_cast"].n
        stat_base = g["s_stat"].n

        def s_cast_op(sc, cast_base=cast_base, stat_base=stat_base, gate_waits=list(gate_waits)):
            for dc in range(NDC):
                if dc == 0:
                    for (sh, n) in gate_waits:
                        sc.wait_ge(sh, n)
                if dc >= 2:
                    sc.wait_ge(g["s_stat"].h, stat_base + (dc - 1) * 2)
                sc.activation(ab(dc % 2), x[:, dc * T:(dc + 1) * T], AF.Copy)
                sc.activation(sq(dc % 2), x[:, dc * T:(dc + 1) * T], AF.Square).then_inc(g["s_cast"].h, 1)
        PS.append(s_cast_op)
        g["s_cast"].bump(NDC)

        bw0 = []
        for k in range(4):
            bw0 += bank_wait_t(k)

        def t_stat(t, cast_base=cast_base, bw0=bw0):
            for (sh, n) in bw0:
                t.wait_ge(sh, n)
            for dc in range(NDC):
                t.wait_ge(g["s_cast"].h, cast_base + dc + 1)
                for th in range(NTH):
                    t.matmul(psum[0:1, (2 * th) * 512:(2 * th) * 512 + 512],
                             c2048[:, :], ab(dc % 2)[:, th * TH:(th + 1) * TH],
                             start=(dc == 0), stop=(dc == NDC - 1))
                    inst = t.matmul(psum[0:1, (2 * th + 1) * 512:(2 * th + 1) * 512 + 512],
                                    c2048[:, :], sq(dc % 2)[:, th * TH:(th + 1) * TH],
                                    start=(dc == 0), stop=(dc == NDC - 1))
                    if th == 1:
                        inst.then_inc(g["s_stat"].h, 2)
        PT.append(t_stat)
        g["s_stat"].bump(2 * NDC)
        stat_all = g["s_stat"].n

        row_base = g["s_lnrow"].n

        def v_chain1(v, stat_all=stat_all):
            v.wait_ge(g["s_stat"].h, stat_all)
            for th in range(NTH):
                mean_r = psum[0:1, (2 * th) * 512:(2 * th) * 512 + 512]
                msq_r = psum[0:1, (2 * th + 1) * 512:(2 * th + 1) * 512 + 512]
                mu_sb = rowpool[0:1, th * 512:(th + 1) * 512]
                scr = rowpool[0:1, 1024 + th * 512:1024 + (th + 1) * 512]
                v.tensor_copy(mu_sb, mean_r)
                v.drain()
                v.tensor_tensor(scr, mu_sb, mu_sb, ALU.mult)
                v.drain()
                v.tensor_tensor(msq_r, msq_r, scr, ALU.subtract).then_inc(g["s_lnrow"].h, 1)
        PV.append(v_chain1)
        g["s_lnrow"].bump(NTH)

        sq_base = g["s_lnsq"].n

        def s_sqrt(sc, row_base=row_base):
            for th in range(NTH):
                sc.wait_ge(g["s_lnrow"].h, row_base + th + 1)
                msq_r = psum[0:1, (2 * th + 1) * 512:(2 * th + 1) * 512 + 512]
                sc.activation(msq_r, msq_r, AF.Sqrt, bias=EPS).then_inc(g["s_lnsq"].h, 1)
        PS.append(s_sqrt)
        g["s_lnsq"].bump(NTH)
        chain2 = g["s_lnrow"].n

        def v_recip(v, sq_base=sq_base):
            for th in range(NTH):
                v.wait_ge(g["s_lnsq"].h, sq_base + th + 1)
                msq_r = psum[0:1, (2 * th + 1) * 512:(2 * th + 1) * 512 + 512]
                v.reciprocal(rowpool[0:1, 1024 + th * 512: 1024 + (th + 1) * 512], msq_r).then_inc(g["s_lnrow"].h, 1)
        PV.append(v_recip)
        g["s_lnrow"].bump(NTH)

        lnb_base = g["s_lnb"].n
        bw2 = []
        for k in range(4, 8):
            bw2 += bank_wait_t(k)

        def t_bcast(t, chain2=chain2, bw2=bw2):
            for (sh, n) in bw2:
                t.wait_ge(sh, n)
            for th in range(NTH):
                t.wait_ge(g["s_lnrow"].h, chain2 + th + 1)
                mu_sl = rowpool[0:1, th * 512:(th + 1) * 512]
                inv_sl = rowpool[0:1, 1024 + th * 512:1024 + (th + 1) * 512]
                t.matmul(BANK(4 + th * 2), onesrow[:, :], mu_sl, start=True, stop=True)
                inst = t.matmul(BANK(5 + th * 2), onesrow[:, :], inv_sl, start=True, stop=True)
                inst.then_inc(g["s_lnb"].h, 1)
        PT.append(t_bcast)
        g["s_lnb"].bump(NTH)

        lnx_base0 = g["s_lnx"].n

        def v_apply(v, lnb_base=lnb_base):
            for th in range(NTH):
                v.wait_ge(g["s_lnb"].h, lnb_base + th + 1)
                for dc in range(NDC):
                    lx = arena[:, OFF_LNX + dc * T + th * TH: OFF_LNX + dc * T + (th + 1) * TH]
                    v.tensor_tensor(lx, x[:, dc * T + th * TH: dc * T + (th + 1) * TH],
                                    BANK(4 + th * 2), ALU.subtract)
                    v.drain()
                    v.tensor_tensor(lx, lx, BANK(5 + th * 2), ALU.mult).then_inc(g["s_lnx"].h, 1)
        PV.append(v_apply)
        g["s_lnx"].bump(NDC * NTH)
        for k in range(8):
            set_bank(k, g["s_lnx"], g["s_lnx"].n)
        return (lnx_base0 + NDC, lnx_base0 + 2 * NDC)

    def ph_router(lnx_all):
        rt_base = g["s_rt"].n
        bw6 = bank_wait_t(6)

        def t_logits(t, lnx_all=lnx_all, bw6=bw6):
            for (sh, n) in bw6:
                t.wait_ge(sh, n)
            t.wait_ge(g["s_wrb"].h, 16)
            for tq in range(8):
                if tq == 0:
                    t.wait_ge(g["s_lnx"].h, lnx_all[0])
                if tq == 4:
                    t.wait_ge(g["s_lnx"].h, lnx_all[1])
                pr = psum[:, 6 * 512 + tq * 8: 6 * 512 + (tq + 1) * 8]
                inst = None
                for dc in range(NDC):
                    inst = t.matmul(pr, arena[:, OFF_LNX + dc * T + tq * 128: OFF_LNX + dc * T + (tq + 1) * 128],
                                    wrb[:, dc * 8:(dc + 1) * 8],
                                    start=(dc == 0), stop=(dc == NDC - 1))
                inst.then_inc(g["s_rt"].h, 1)
        PT.append(t_logits)
        g["s_rt"].bump(8)

        rw_base = g["s_rw"].n
        W = wbscr

        def v_rs1(v, rt_base=rt_base):
            v.wait_ge(g["s_rt"].h, rt_base + 8)
            for tq in range(8):
                pr = psum[:, 6 * 512 + tq * 8: 6 * 512 + (tq + 1) * 8]
                v.tensor_reduce(W[:, tq:tq + 1], pr, AX, ALU.max, negate=True)
            v.drain()
            v.engine_nop().then_inc(g["s_rw"].h, 1)
        PV.append(v_rs1)
        g["s_rw"].bump()

        def s_rexp(sc, rw_base=rw_base):
            sc.wait_ge(g["s_rw"].h, rw_base + 1)
            inst = None
            for tq in range(8):
                pr = psum[:, 6 * 512 + tq * 8: 6 * 512 + (tq + 1) * 8]
                inst = sc.activation(W[:, 64 + tq * 8:64 + (tq + 1) * 8], pr, AF.Exp,
                                     bias=W[:, tq:tq + 1])
            inst.then_inc(g["s_rw"].h, 1)
        PS.append(s_rexp)
        g["s_rw"].bump()

        def v_rs2(v, rw_base=rw_base):
            v.wait_ge(g["s_rw"].h, rw_base + 2)
            for tq in range(8):
                v.reduce_sum(W[:, 24 + tq:25 + tq], W[:, 64 + tq * 8:64 + (tq + 1) * 8], AX)
            v.drain()
            v.reciprocal(W[:, 32:40], W[:, 24:32])
            v.drain()
            for tq in range(8):
                v.tensor_scalar(W[:, 128 + tq * 8:128 + (tq + 1) * 8],
                                W[:, 64 + tq * 8:64 + (tq + 1) * 8],
                                W[:, 32 + tq:33 + tq], None, ALU.mult)
            v.drain()
            for tq in range(8):
                v.reduce_max(W[:, 8 + tq:9 + tq], W[:, 128 + tq * 8:128 + (tq + 1) * 8], AX)
            v.drain()
            for tq in range(8):
                v.tensor_scalar(W[:, 192 + tq * 8:192 + (tq + 1) * 8],
                                W[:, 128 + tq * 8:128 + (tq + 1) * 8],
                                W[:, 8 + tq:9 + tq], None, ALU.is_equal)
            v.drain()
            for tq in range(8):
                v.scalar_tensor_tensor(W[:, 192 + tq * 8:192 + (tq + 1) * 8],
                                       W[:, 192 + tq * 8:192 + (tq + 1) * 8], -1.0e30,
                                       W[:, 128 + tq * 8:128 + (tq + 1) * 8], ALU.mult, ALU.add)
            v.drain()
            for tq in range(8):
                v.reduce_max(W[:, 16 + tq:17 + tq], W[:, 192 + tq * 8:192 + (tq + 1) * 8], AX)
            v.drain()
            for tq in range(8):
                v.tensor_scalar(W[:, 256 + tq * 8:256 + (tq + 1) * 8],
                                W[:, 128 + tq * 8:128 + (tq + 1) * 8],
                                W[:, 16 + tq:17 + tq], None, ALU.is_ge)
            v.drain()
            for tq in range(8):
                v.tensor_tensor(wtok[:, tq:tq + 57:8], W[:, 128 + tq * 8:128 + (tq + 1) * 8],
                                W[:, 256 + tq * 8:256 + (tq + 1) * 8], ALU.mult)
            v.drain()
            v.engine_nop().then_inc(g["s_rw"].h, 1)
        PV.append(v_rs2)
        g["s_rw"].bump()
        rw_all = g["s_rw"].n

        bw7 = bank_wait_t(7)

        def t_wtr(t, rw_all=rw_all, bw7=bw7):
            t.wait_ge(g["s_rw"].h, rw_all)
            for (sh, n) in bw7:
                t.wait_ge(sh, n)
            inst = t.transpose(psum[0:64, 7 * 512:7 * 512 + 128], wtok[:, :], identf[:, :])
            inst.then_inc(g["s_rwt"].h, 1)
        PT.append(t_wtr)
        g["s_rwt"].bump()
        rwt_all = g["s_rwt"].n

        def v_wev(v, rwt_all=rwt_all):
            v.wait_ge(g["s_rwt"].h, rwt_all)
            v.tensor_copy(wsb[:, :], psum[0:64, 7 * 512:7 * 512 + 128]).then_inc(g["s_rwe"].h, 1)
        PV.append(v_wev)
        g["s_rwe"].bump()
        set_bank(6, g["s_rw"], rw_all)
        set_bank(7, g["s_rwe"], g["s_rwe"].n)

    ftctr = [0]
    zctr = [0]

    def ph_expert(e_idx, pass_i):
        shared = (e_idx == 0)
        nft = NFT_S if shared else NFT_E
        if shared:
            gsrc = lambda ft: wsg_s[ft, :, :]
            usrc = lambda ft: wsu_s[ft, :, :]
            dsrc = lambda dt_i, half: wsd_s[dt_i, :, half * 1408:(half + 1) * 1408]
        else:
            e = e_idx - 1
            gsrc = lambda ft, e=e: wg_s[e, ft, :, :]
            usrc = lambda ft, e=e: wu_s[e, ft, :, :]
            dsrc = lambda dt_i, half, e=e: wd_s[e, dt_i, :, :]

        hgate = g["s_z"].n
        if not shared:
            e = e_idx - 1
            slot = e % 2
            rwe_tgt = g["s_rwe"].n
            wb_need = g["s_wb"].n - 2 if e >= 2 else None

            rext_pre = g["s_rext"].n

            def y_ext(y, slot=slot, e=e, rwe_tgt=rwe_tgt, wb_need=wb_need, rext_pre=rext_pre):
                y.wait_ge(g["s_rwe"].h, rwe_tgt)
                if rext_pre > 0:
                    y.wait_ge(g["s_rext"].h, rext_pre)
                if wb_need is not None:
                    y.wait_ge(g["s_wb"].h, wb_need + 2)
                y.dma_start(out=rowpool[0:1, slot * 1024:(slot + 1) * 1024],
                            in_=wsb[e * 8:(e + 1) * 8, :]).then_inc(g["s_rext"].h, 16)
            PY.append(y_ext)
            g["s_rext"].bump(16)
            rext_tgt = g["s_rext"].n
            wb_base = g["s_wb"].n
            bw6a = bank_wait_t(6)
            wbe_pre = g["s_wbe"].n

            def t_wb(t, slot=slot, rext_tgt=rext_tgt, bw6a=bw6a, wbe_pre=wbe_pre):
                t.wait_ge(g["s_rext"].h, rext_tgt)
                for (sh, n) in bw6a:
                    t.wait_ge(sh, n)
                for th in range(NTH):
                    if th == 1:
                        t.wait_ge(g["s_wbe"].h, wbe_pre + 1)
                    inst = t.matmul(BANK(6), onesrow[:, :],
                                    rowpool[0:1, slot * 1024 + th * 512: slot * 1024 + (th + 1) * 512],
                                    start=True, stop=True)
                    inst.then_inc(g["s_wb"].h, 1)
            PT.append(t_wb)
            g["s_wb"].bump(NTH)

            def v_wb(v, wb_base=wb_base):
                for th in range(NTH):
                    v.wait_ge(g["s_wb"].h, wb_base + th + 1)
                    v.tensor_copy(wbscr[:, th * 512:(th + 1) * 512], BANK(6)).then_inc(g["s_wbe"].h, 1)
            PV.append(v_wb)
            g["s_wbe"].bump(NTH)
            set_bank(6, g["s_wbe"], g["s_wbe"].n)
        wbe_tgt = g["s_wbe"].n

        for ft in range(nft):
            cg = load_wchunk(gsrc(ft), 2048)
            cu = load_wchunk(usrc(ft), 2048)
            B0 = (ftctr[0] % 2) * 4
            ftctr[0] += 1
            fft_base = g["s_fft"].n
            bws = [bank_wait_t(k) for k in range(B0, B0 + 4)]

            def t_up(t, cg=cg, cu=cu, B0=B0, bws=bws):
                t.wait_ge(cg[1], cg[2])
                for bw in bws:
                    for (sh, n) in bw:
                        t.wait_ge(sh, n)
                inst = None
                for dc in range(NDC):
                    for th in range(NTH):
                        inst = t.matmul(BANK(B0 + th), wap(cg, dc * 128, (dc + 1) * 128),
                                        arena[:, OFF_LNX + dc * T + th * TH: OFF_LNX + dc * T + (th + 1) * TH],
                                        start=(dc == 0), stop=(dc == NDC - 1))
                inst.then_inc(g["s_fft"].h, 1)
                t.wait_ge(cu[1], cu[2])
                for dc in range(NDC):
                    for th in range(NTH):
                        inst = t.matmul(BANK(B0 + 2 + th), wap(cu, dc * 128, (dc + 1) * 128),
                                        arena[:, OFF_LNX + dc * T + th * TH: OFF_LNX + dc * T + (th + 1) * TH],
                                        start=(dc == 0), stop=(dc == NDC - 1))
                inst.then_inc(g["s_fft"].h, 1)
            PT.append(t_up)
            g["s_fft"].bump(2)
            chunk_free_at(cg, g["s_fft"], fft_base + 1)
            chunk_free_at(cu, g["s_fft"], fft_base + 2)
            fslot = ft % 2
            sg_base = g["s_sg"].n

            def s_silu(sc, B0=B0, fft_base=fft_base, fslot=fslot):
                sc.wait_ge(g["s_fft"].h, fft_base + 1)
                for th in range(NTH):
                    sc.activation(sgscr[:, (fslot * 2 + th) * TH:(fslot * 2 + th + 1) * TH],
                                  BANK(B0 + th), AF.Silu).then_inc(g["s_sg"].h, 1)
            PS.append(s_silu)
            g["s_sg"].bump(NTH)
            sg_now = g["s_sg"].n
            h_base = g["s_h"].n

            def v_h(v, B0=B0, ft=ft, fft_base=fft_base, sg_now=sg_now, fslot=fslot,
                    shared=shared, hgate=hgate, wbe_tgt=wbe_tgt):
                v.wait_ge(g["s_fft"].h, fft_base + 2)
                v.wait_ge(g["s_sg"].h, sg_now)
                if ft == 0:
                    if hgate > 0:
                        v.wait_ge(g["s_z"].h, hgate)
                    if not shared:
                        v.wait_ge(g["s_wbe"].h, wbe_tgt)
                for th in range(NTH):
                    hb = arena[:, OFF_HB + ft * T + th * TH: OFF_HB + ft * T + (th + 1) * TH]
                    if shared:
                        v.tensor_tensor(hb, sgscr[:, (fslot * 2 + th) * TH:(fslot * 2 + th + 1) * TH],
                                        BANK(B0 + 2 + th), ALU.mult).then_inc(g["s_h"].h, 1)
                    else:
                        v.drain()
                        v.tensor_tensor(uwscr[:, :], BANK(B0 + 2 + th),
                                        wbscr[:, th * 512:(th + 1) * 512], ALU.mult)
                        v.drain()
                        v.tensor_tensor(hb, sgscr[:, (fslot * 2 + th) * TH:(fslot * 2 + th + 1) * TH],
                                        uwscr[:, :], ALU.mult).then_inc(g["s_h"].h, 1)
            PV.append(v_h)
            g["s_h"].bump(NTH)
            set_bank(B0, g["s_sg"], sg_base + 1)
            set_bank(B0 + 1, g["s_sg"], sg_base + 2)
            set_bank(B0 + 2, g["s_h"], h_base + 1)
            set_bank(B0 + 3, g["s_h"], h_base + 2)

        h_all = g["s_h"].n
        for dt_i in range(16):
            if shared:
                chunks = [(load_wchunk(dsrc(dt_i, 0), 1408), 0, 11),
                          (load_wchunk(dsrc(dt_i, 1), 1408), 11, 22)]
            else:
                chunks = [(load_wchunk(dsrc(dt_i, 0), 1408), 0, 11)]
            zb = [(zctr[0] + k) % 8 for k in range(NTH)]
            zctr[0] += NTH
            z_base = g["s_z"].n
            bwz = [bank_wait_t(zb[th]) for th in range(NTH)]

            def t_dn(t, chunks=chunks, dt_i=dt_i, h_all=h_all, zb=zb, bwz=bwz):
                for ch, _, _ in chunks:
                    t.wait_ge(ch[1], ch[2])
                if dt_i == 0:
                    t.wait_ge(g["s_h"].h, h_all)
                for th in range(NTH):
                    for (sh, n) in bwz[th]:
                        t.wait_ge(sh, n)
                    first = True
                    inst = None
                    for ci, (ch, fc0, fc1) in enumerate(chunks):
                        for fc in range(fc0, fc1):
                            inst = t.matmul(BANK(zb[th]),
                                            wap(ch, (fc - fc0) * 128, (fc - fc0 + 1) * 128),
                                            arena[:, OFF_HB + fc * T + th * TH: OFF_HB + fc * T + (th + 1) * TH],
                                            start=first, stop=(ci == len(chunks) - 1 and fc == fc1 - 1))
                            first = False
                    inst.then_inc(g["s_z"].h, 1)
            PT.append(t_dn)
            g["s_z"].bump(NTH)
            for ch, _, _ in chunks:
                chunk_free_at(ch, g["s_z"], z_base + NTH)

            def v_dn(v, dt_i=dt_i, z_base=z_base, zb=zb):
                for th in range(NTH):
                    v.wait_ge(g["s_z"].h, z_base + th + 1)
                    xs = x[:, dt_i * T + th * TH: dt_i * T + (th + 1) * TH]
                    v.scalar_tensor_tensor(xs, BANK(zb[th]), 0.5, xs, ALU.mult, ALU.add).then_inc(g["s_zf"].h, 1)
            PV.append(v_dn)
            for th in range(NTH):
                g["s_zf"].bump()
                set_bank(zb[th], g["s_zf"], g["s_zf"].n)


    def ph_qkv(lnx_all):
        LNX = lambda dc, a, b: arena[:, OFF_LNX + dc * T + a: OFF_LNX + dc * T + b]

        knorm_pending = [None]

        def flush_knorm():
            if knorm_pending[0] is not None:
                tf, vf = knorm_pending[0]
                PT.append(tf)
                PV.append(vf)
                knorm_pending[0] = None

        for ft in range(16):
            ck = load_wchunk(wk_s[ft, :, :], 2048)
            B0 = (ft % 2) * 2
            kp_base = g["s_kproj"].n
            ke_base = g["s_ke"].n
            ks_need = (s_ks[ft % 2].h, s_ks[ft % 2].n) if ft >= 2 else None
            bwk = bank_wait_t(B0) + bank_wait_t(B0 + 1)

            def t_k(t, ck=ck, B0=B0, ft=ft, lnx_all=lnx_all, bwk=bwk):
                t.wait_ge(ck[1], ck[2])
                for (sh, n) in bwk:
                    t.wait_ge(sh, n)
                inst = None
                for th in range(NTH):
                    if ft == 0:
                        t.wait_ge(g["s_lnx"].h, lnx_all[th])
                    for dc in range(NDC):
                        inst = t.matmul(BANK(B0 + th), wap(ck, dc * 128, (dc + 1) * 128),
                                        LNX(dc, th * TH, (th + 1) * TH),
                                        start=(dc == 0), stop=(dc == NDC - 1))
                inst.then_inc(g["s_kproj"].h, 2)
            PT.append(t_k)
            g["s_kproj"].bump(2)
            chunk_free_at(ck, g["s_kproj"], kp_base + 2)

            def v_k(v, B0=B0, ft=ft, kp_base=kp_base, ks_need=ks_need):
                v.wait_ge(g["s_kproj"].h, kp_base + 2)
                if ks_need is not None:
                    v.wait_ge(ks_need[0], ks_need[1])
                for th in range(NTH):
                    v.tensor_copy(arena[:, OFF_KST + (ft % 2) * 1024 + th * TH: OFF_KST + (ft % 2) * 1024 + (th + 1) * TH],
                                  BANK(B0 + th)).then_inc(g["s_ke"].h, 1)
            PV.append(v_k)
            g["s_ke"].bump(2)
            set_bank(B0, g["s_ke"], ke_base + 1)
            set_bank(B0 + 1, g["s_ke"], ke_base + 2)
            ke_tgt = g["s_ke"].n
            kn1_base = g["s_kn1"].n
            kn2_base = g["s_kn2"].n

            def s_knsq(sc, ft=ft, ke_tgt=ke_tgt, kn2_base=kn2_base):
                sc.wait_ge(g["s_ke"].h, ke_tgt)
                if kn2_base > 0:
                    sc.wait_ge(g["s_kn2"].h, kn2_base)
                sc.activation(nscr[:, :],
                              arena[:, OFF_KST + (ft % 2) * 1024: OFF_KST + (ft % 2) * 1024 + 1024],
                              AF.Square).then_inc(g["s_kn1"].h, 1)
            PS.append(s_knsq)
            g["s_kn1"].bump()
            flush_knorm()
            bw45 = bank_wait_t(4) + bank_wait_t(5)
            kn1_tgt = g["s_kn1"].n

            def t_knmm(t, kn1_tgt=kn1_tgt, bw45=bw45):
                t.wait_ge(g["s_kn1"].h, kn1_tgt)
                for (sh, n) in bw45:
                    t.wait_ge(sh, n)
                t.matmul(psum[0:1, 4 * 512:4 * 512 + 512], onescol[:, :],
                         nscr[:, 0:512], start=True, stop=True)
                inst = t.matmul(psum[0:1, 5 * 512:5 * 512 + 512], onescol[:, :],
                                nscr[:, 512:1024], start=True, stop=True)
                inst.then_inc(g["s_kn2"].h, 1)
            g["s_kn2"].bump()
            kn2_tgt = g["s_kn2"].n

            def v_knred(v, ft=ft, kn2_tgt=kn2_tgt):
                v.wait_ge(g["s_kn2"].h, kn2_tgt)
                v.reduce_max(kq_sb[0:1, ft:ft + 1], psum[0:1, 4 * 512:4 * 512 + 1024], AX).then_inc(g["s_kn2"].h, 1)
            g["s_kn2"].bump()
            knorm_pending[0] = (t_knmm, v_knred)
            set_bank(4, g["s_kn2"], g["s_kn2"].n)
            set_bank(5, g["s_kn2"], g["s_kn2"].n)

            def y_k(y, ft=ft, ke_tgt=ke_tgt):
                y.wait_ge(g["s_ke"].h, ke_tgt)
                y.dma_start(out=k_send[ft * 131072: (ft + 1) * 131072].rearrange("(p t) -> p t", t=T),
                            in_=arena[:, OFF_KST + (ft % 2) * 1024: OFF_KST + (ft % 2) * 1024 + 1024]).then_inc(s_ks[ft % 2].h, 16)
            PY.append(y_k)
            s_ks[ft % 2].bump(16)
        flush_knorm()
        ksend_all = (s_ks[0].n, s_ks[1].n)

        kn_all = g["s_kn2"].n

        def y_kn(y, kn_all=kn_all):
            y.wait_ge(g["s_kn2"].h, kn_all)
            y.dma_start(out=k_send[2097152:2097168].rearrange("(a b) -> a b", a=1),
                        in_=kq_sb[0:1, :]).then_inc(g["s_knl"].h, 16)
        PY.append(y_kn)
        g["s_knl"].bump(16)
        knl_send = g["s_knl"].n

        def g_ccK(gp, ksend_all=ksend_all, knl_send=knl_send):
            gp.wait_ge(s_ks[0].h, ksend_all[0])
            gp.wait_ge(s_ks[1].h, ksend_all[1])
            gp.wait_ge(g["s_knl"].h, knl_send)
            gp.collective_compute(
                "AllGather", ALU.bypass,
                replica_groups=[[0, 1, 2, 3], [4, 5, 6, 7]],
                ins=[k_send], outs=[k_gath],
            ).then_inc(g["s_cc"].h, 1)
        PG.append(g_ccK)
        g["s_cc"].bump()

        for fcol in range(4):
            vp_base = g["s_vproj"].n
            chunks = [load_wchunk(wv_s[fcol, dcg, :, :], 2048) for dcg in range(3)]
            for dcg in range(3):
                chunk_free_at(chunks[dcg], g["s_vproj"], vp_base + dcg + 1)
            chunks.append(load_wchunk(wv_s[fcol, 3, :, :], 2048))
            bwv = [bank_wait_t(k) for k in range(8)]

            def t_v(t, chunks=chunks, bwv=bwv):
                inst = None
                for dcg in range(4):
                    ch = chunks[dcg]
                    t.wait_ge(ch[1], ch[2])
                    if dcg == 0:
                        for bw in bwv:
                            for (sh, n) in bw:
                                t.wait_ge(sh, n)
                    for dci in range(4):
                        dc = dcg * 4 + dci
                        for tk in range(8):
                            inst = t.matmul(BANK(tk), LNX(dc, tk * 128, (tk + 1) * 128),
                                            wap(ch, dci * 512, (dci + 1) * 512),
                                            start=(dc == 0), stop=(dc == NDC - 1))
                            if dc == NDC - 1:
                                inst.then_inc(g["s_vproj"].h, 1)
                            elif dci == 3 and tk == 7:
                                inst.then_inc(g["s_vproj"].h, 1)
            PT.append(t_v)
            g["s_vproj"].bump(11)
            chunk_free_at(chunks[3], g["s_vproj"], vp_base + 11)
            ve_base = g["s_ve"].n

            def v_v(v, fcol=fcol, vp_base=vp_base):
                v.wait_ge(g["s_vproj"].h, vp_base + 11)
                for tk in range(8):
                    inst = None
                    for hh in range(4):
                        h = fcol * 4 + hh
                        inst = v.tensor_copy(arena[:, OFF_VSB + h * 1024 + tk * 128: OFF_VSB + h * 1024 + (tk + 1) * 128],
                                             psum[:, tk * 512 + hh * 128: tk * 512 + (hh + 1) * 128])
                    inst.then_inc(g["s_ve"].h, 1)
            PV.append(v_v)
            g["s_ve"].bump(8)
            for k in range(8):
                set_bank(k, g["s_ve"], ve_base + k + 1)
        ve_all = g["s_ve"].n

        def y_v(y, ve_all=ve_all):
            y.wait_ge(g["s_ve"].h, ve_all)
            y.dma_start(out=v_send[:].rearrange("(p w) -> p w", w=16384),
                        in_=arena[:, OFF_VSB:OFF_VSB + 16384]).then_inc(g["s_vsend"].h, 16)
        PY.append(y_v)
        g["s_vsend"].bump(16)
        vsend_all = g["s_vsend"].n

        def g_ccV(gp, vsend_all=vsend_all):
            gp.wait_ge(g["s_vsend"].h, vsend_all)
            gp.collective_compute(
                "AllGather", ALU.bypass,
                replica_groups=[[0, 1, 2, 3], [4, 5, 6, 7]],
                ins=[v_send], outs=[v_gath],
            ).then_inc(g["s_cc"].h, 1)
        PG.append(g_ccV)
        g["s_cc"].bump()

        qnorm_pending = [None]

        def flush_qnorm():
            if qnorm_pending[0] is not None:
                tf, vf = qnorm_pending[0]
                PT.append(tf)
                PV.append(vf)
                qnorm_pending[0] = None

        for ft in range(16):
            cq = load_wchunk(wq_s[ft, :, :], 2048)
            B0 = (ft % 2) * 2
            qp_base = g["s_qproj"].n
            qe_base = g["s_qe"].n
            bwq = bank_wait_t(B0) + bank_wait_t(B0 + 1)

            def t_q(t, cq=cq, B0=B0, bwq=bwq):
                t.wait_ge(cq[1], cq[2])
                for (sh, n) in bwq:
                    t.wait_ge(sh, n)
                inst = None
                for dc in range(NDC):
                    for th in range(NTH):
                        inst = t.matmul(BANK(B0 + th), wap(cq, dc * 128, (dc + 1) * 128),
                                        LNX(dc, th * TH, (th + 1) * TH),
                                        start=(dc == 0), stop=(dc == NDC - 1))
                inst.then_inc(g["s_qproj"].h, 2)
            PT.append(t_q)
            g["s_qproj"].bump(2)
            chunk_free_at(cq, g["s_qproj"], qp_base + 2)

            def v_q(v, B0=B0, ft=ft, qp_base=qp_base):
                v.wait_ge(g["s_qproj"].h, qp_base + 2)
                for th in range(NTH):
                    v.tensor_copy(arena[:, OFF_QT + ft * T + th * TH: OFF_QT + ft * T + (th + 1) * TH],
                                  BANK(B0 + th)).then_inc(g["s_qe"].h, 1)
            PV.append(v_q)
            g["s_qe"].bump(2)
            set_bank(B0, g["s_qe"], qe_base + 1)
            set_bank(B0 + 1, g["s_qe"], qe_base + 2)
            qe_tgt2 = g["s_qe"].n
            qn1_base = g["s_qn1"].n
            qn2_base = g["s_qn2"].n

            def s_qnsq(sc, ft=ft, qe_tgt2=qe_tgt2, qn2_base=qn2_base):
                sc.wait_ge(g["s_qe"].h, qe_tgt2)
                if qn2_base > 0:
                    sc.wait_ge(g["s_qn2"].h, qn2_base)
                sc.activation(nscr[:, :],
                              arena[:, OFF_QT + ft * T: OFF_QT + ft * T + 1024],
                              AF.Square).then_inc(g["s_qn1"].h, 1)
            PS.append(s_qnsq)
            g["s_qn1"].bump()
            flush_qnorm()
            bw45q = bank_wait_t(4) + bank_wait_t(5)
            qn1_tgt = g["s_qn1"].n

            def t_qnmm(t, qn1_tgt=qn1_tgt, bw45q=bw45q):
                t.wait_ge(g["s_qn1"].h, qn1_tgt)
                for (sh, n) in bw45q:
                    t.wait_ge(sh, n)
                t.matmul(psum[0:1, 4 * 512:4 * 512 + 512], onescol[:, :],
                         nscr[:, 0:512], start=True, stop=True)
                inst = t.matmul(psum[0:1, 5 * 512:5 * 512 + 512], onescol[:, :],
                                nscr[:, 512:1024], start=True, stop=True)
                inst.then_inc(g["s_qn2"].h, 1)
            g["s_qn2"].bump()
            qn2_tgt = g["s_qn2"].n

            def v_qnred(v, ft=ft, qn2_tgt=qn2_tgt):
                v.wait_ge(g["s_qn2"].h, qn2_tgt)
                for qb in range(2):
                    v.reduce_max(qn_sb[0:1, ft * 2 + qb:ft * 2 + qb + 1],
                                 psum[0:1, (4 + qb) * 512:(4 + qb) * 512 + 512], AX).then_inc(g["s_qn2"].h, 1)
            g["s_qn2"].bump(2)
            qnorm_pending[0] = (t_qnmm, v_qnred)
            set_bank(4, g["s_qn2"], g["s_qn2"].n)
            set_bank(5, g["s_qn2"], g["s_qn2"].n)
        flush_qnorm()
        return g["s_qe"].n

    def ph_attn(qe_all):
        cc_tgt = g["s_cc"].n
        ccK_tgt = cc_tgt - 1
        vsend_all_gate = g["s_vsend"].n
        qe_gate = qe_all
        kslot_free = {0: None, 1: None, 2: None}
        vslot_free = {0: None, 1: None}
        first_load = [True]
        first_vload = [True]
        first_c = [True]
        vsb_view = v_send[:].rearrange("(p h k d) -> p h k d", h=16, k=8, d=128)
        KSL = 4096

        knl_pre = g["s_knl"].n

        def y_knb(y, knl_pre=knl_pre):
            y.wait_ge(g["s_cc"].h, ccK_tgt)
            y.dma_start(out=knb_sb[0:1, :].rearrange("a (r c) -> a r c", r=4),
                        in_=k_gath[:, 2097152:2097168].rearrange("r (a c) -> a r c", a=1)
                        ).then_inc(g["s_knl"].h, 16)
        PY.append(y_knb)
        g["s_knl"].bump(16)
        knl_tgt = g["s_knl"].n

        def emit_kload(h, qb):
            slot = h % 3
            sl = s_kl[slot]
            base = OFF_KB + slot * KSL
            need = kslot_free[slot]
            isfirst = first_load[0]
            first_load[0] = False

            def y_op(y, h=h, qb=qb, base=base, sl=sl, need=need, isfirst=isfirst):
                if isfirst:
                    y.wait_ge(g["s_cc"].h, ccK_tgt)
                    y.wait_ge(g["s_vsend"].h, vsend_all_gate)
                if need is not None:
                    y.wait_ge(need[0], need[1])
                y.dma_start(out=arena[:, base:base + 512],
                            in_=k_send[h * 131072: (h + 1) * 131072]
                            .rearrange("(p t) -> p t", t=T)[:, qb * 512:(qb + 1) * 512]).then_inc(sl.h, 16)
                gk = k_gath[:, h * 131072: (h + 1) * 131072].rearrange(
                    "r (p t) -> p r t", t=T)
                if qb == 0:
                    y.dma_start(out=arena[:, base + 512:base + 2048].rearrange("p (r t) -> p r t", t=512),
                                in_=gk[:, 0:3, 0:512]).then_inc(sl.h, 16)
                else:
                    y.dma_start(out=arena[:, base + 512:base + 2560].rearrange("p (r t) -> p r t", t=512),
                                in_=gk[:, :, 0:512]).then_inc(sl.h, 16)
                    y.dma_start(out=arena[:, base + 2560:base + 4096].rearrange("p (r t) -> p r t", t=512),
                                in_=gk[:, 1:4, 512:1024]).then_inc(sl.h, 16)
            PY.append(y_op)
            sl.bump(16 * (2 if qb == 0 else 3))
            return slot, sl.h, sl.n

        def emit_vload(h, qb):
            slot = h % 2
            sl = s_vl[slot]
            base = OFF_VB + slot * KSL
            need = vslot_free[slot]
            isfirst = first_vload[0]
            first_vload[0] = False

            def y_op(y, h=h, qb=qb, base=base, sl=sl, need=need, isfirst=isfirst):
                if isfirst:
                    y.wait_ge(g["s_qe"].h, qe_gate)
                    y.wait_ge(g["s_cc"].h, cc_tgt)
                if need is not None:
                    y.wait_ge(need[0], need[1])
                y.dma_start(out=arena[:, base:base + 512].rearrange("p (k d) -> p k d", d=128),
                            in_=vsb_view[:, h, qb * 4:(qb + 1) * 4, :]).then_inc(sl.h, 16)
                gv = v_gath[:, :].rearrange("r (p h k d) -> p r h k d", h=16, k=8, d=128)
                if qb == 0:
                    y.dma_start(out=arena[:, base + 512:base + 2048].rearrange("p (r k d) -> p r k d", k=4, d=128),
                                in_=gv[:, 0:3, h, 0:4, :]).then_inc(sl.h, 16)
                else:
                    y.dma_start(out=arena[:, base + 512:base + 2560].rearrange("p (r k d) -> p r k d", k=4, d=128),
                                in_=gv[:, :, h, 0:4, :]).then_inc(sl.h, 16)
                    y.dma_start(out=arena[:, base + 2560:base + 4096].rearrange("p (r k d) -> p r k d", k=4, d=128),
                                in_=gv[:, 1:4, h, 4:8, :]).then_inc(sl.h, 16)
            PY.append(y_op)
            sl.bump(16 * (2 if qb == 0 else 3))
            return slot, sl.h, sl.n

        def emit_c(h, qb):
            cslot = h % 2
            exp_now = g["s_exp"].n
            isfirst = first_c[0]
            first_c[0] = False

            def v_c(v, h=h, qb=qb, cslot=cslot, isfirst=isfirst):
                if isfirst:
                    v.wait_ge(g["s_knl"].h, knl_tgt)
                v.reduce_max(kmx[0:1, cslot:cslot + 1],
                             knb_sb[0:1, :].rearrange("a (r c) -> a r c", r=4)[:, :, h], AX)
                v.drain()
                v.tensor_tensor(cqs[0:1, cslot:cslot + 1], kmx[0:1, cslot:cslot + 1],
                                qn_sb[0:1, h * 2 + qb:h * 2 + qb + 1], ALU.mult).then_inc(g["s_cq"].h, 1)
            PV.append(v_c)
            g["s_cq"].bump()
            cq_tgt = g["s_cq"].n

            def s_c(sc, cslot=cslot, cq_tgt=cq_tgt):
                sc.wait_ge(g["s_cq"].h, cq_tgt)
                sc.activation(cscs[0:1, cslot:cslot + 1], cqs[0:1, cslot:cslot + 1],
                              AF.Sqrt, scale=1.21).then_inc(g["s_csc"].h, 1)
            PS.append(s_c)
            g["s_csc"].bump()
            csc_tgt = g["s_csc"].n
            ncb_pre = g["s_sa"].n

            def v_crow(v, cslot=cslot, csc_tgt=csc_tgt, exp_now=exp_now, ncb_pre=ncb_pre):
                v.wait_ge(g["s_csc"].h, csc_tgt)
                if ncb_pre > 0:
                    v.wait_ge(g["s_sa"].h, ncb_pre)
                v.tensor_scalar(crow[0:1, :], negisq[:, :],
                                cscs[0:1, cslot:cslot + 1], None, ALU.mult).then_inc(g["s_cq"].h, 1)
            PV.append(v_crow)
            g["s_cq"].bump()
            crow_tgt = g["s_cq"].n

            def y_ncb(y, cslot=cslot, crow_tgt=crow_tgt, exp_now=exp_now):
                y.wait_ge(g["s_cq"].h, crow_tgt)
                y.wait_ge(g["s_exp"].h, exp_now)
                y.dma_start(out=negc[:, cslot:cslot + 1],
                            in_=crow[0:1, :]).then_inc(g["s_sa"].h, 16)
            PY.append(y_ncb)
            g["s_sa"].bump(16)
            ncb_tgt = g["s_sa"].n

            def v_c3(v, ncb_tgt=ncb_tgt, cslot=cslot, qb=qb, exp_now=exp_now):
                v.wait_ge(g["s_sa"].h, ncb_tgt)
                v.wait_ge(g["s_exp"].h, exp_now)
                ncols = 3 if qb == 0 else 7
                pb = padbias[:, 0:3] if qb == 0 else padbias[:, 3:10]
                v.tensor_scalar(btab[:, cslot * 8:cslot * 8 + ncols], pb, negc[:, cslot:cslot + 1],
                                None, ALU.add).then_inc(g["s_cb"].h, 1)
            PV.append(v_c3)
            g["s_cb"].bump()
            return cslot, g["s_cb"].n

        def emit_B(h, qb, kinfo, vinfo, cinfo):
            kslot, ksemh, ktgt = kinfo
            vslot, vsemh, vtgt = vinfo
            cslot, cb_tgt = cinfo
            kbase = OFF_KB + kslot * KSL
            vbase = OFF_VB + vslot * KSL
            nkb = 4 if qb == 0 else 8
            nkt = nkb * 4
            npair = nkt // 2
            qoff = OFF_QT + h * T + qb * TH
            zbank = 4 + (h % 2)
            sb_base = g["s_sb"].n
            exp_base = g["s_exp"].n
            pf_base = g["s_pf"].n
            inv_prev = g["s_inv"].n
            bwz = bank_wait_t(zbank)
            bw0123 = bank_wait_t(0) + bank_wait_t(1) + bank_wait_t(2) + bank_wait_t(3)

            def t_op(t, h=h, qb=qb, kbase=kbase, vbase=vbase, ksemh=ksemh, ktgt=ktgt,
                     vsemh=vsemh, vtgt=vtgt, cb_tgt=cb_tgt, nkt=nkt, npair=npair,
                     qoff=qoff, zbank=zbank, exp_base=exp_base, inv_prev=inv_prev,
                     bwz=bwz, bw0123=bw0123):
                t.wait_ge(ksemh, ktgt)
                t.wait_ge(vsemh, vtgt)
                t.wait_ge(g["s_cb"].h, cb_tgt)
                for (sh, n) in bwz + bw0123:
                    t.wait_ge(sh, n)
                t.wait_ge(g["s_inv"].h, inv_prev)

                def s_mm(kt):
                    sbk = kt % 4
                    pr = kt // 2
                    if pr >= 2:
                        t.wait_ge(g["s_exp"].h, exp_base + pr - 1)
                    inst = t.matmul(BANK(sbk), arena[:, kbase + kt * 128: kbase + (kt + 1) * 128],
                                    arena[:, qoff:qoff + TH],
                                    start=True, stop=(kt >= 4))
                    if kt < 4:
                        inst = t.matmul(BANK(sbk), identb[:, :],
                                        m0[:, 384 - kt * 128:384 - kt * 128 + 512],
                                        start=False, stop=True)
                    inst.then_inc(g["s_sb"].h, 1)

                def av_mm(kt):
                    t.wait_ge(g["s_exp"].h, exp_base + kt // 2 + 1)
                    pb = arena[:, OFF_PB + (kt % 4) * 512: OFF_PB + (kt % 4 + 1) * 512]
                    t.matmul(BANK(zbank), arena[:, vbase + kt * 128: vbase + (kt + 1) * 128], pb,
                             start=(kt == 0), stop=(kt == nkt - 1))
                    if kt < 4:
                        lcol = onescol[:, 0:1]
                    else:
                        ci = (kt // 4 - 1) + (0 if qb == 0 else 3)
                        lcol = sumind[:, ci:ci + 1]
                    inst = t.matmul(psum[0:1, 6 * 512:6 * 512 + 512], lcol, pb,
                                    start=(kt == 0), stop=(kt == nkt - 1))
                    inst.then_inc(g["s_pf"].h, 1)

                for pr in range(npair):
                    s_mm(2 * pr)
                    s_mm(2 * pr + 1)
                    if pr >= 1:
                        av_mm(2 * pr - 2)
                        av_mm(2 * pr - 1)
                av_mm(nkt - 2)
                av_mm(nkt - 1)
            PT.append(t_op)
            g["s_sb"].bump(nkt)
            g["s_pf"].bump(nkt)

            def s_op(sc, qb=qb, npair=npair, sb_base=sb_base, pf_base=pf_base, cslot=cslot,
                     cb_tgt=cb_tgt):
                for pr in range(npair):
                    sc.wait_ge(g["s_sb"].h, sb_base + 2 * pr + 2)
                    if pr == 0:
                        sc.wait_ge(g["s_cb"].h, cb_tgt)
                    if pr >= 2:
                        sc.wait_ge(g["s_pf"].h, pf_base + 2 * pr - 2)
                    sbk = (2 * pr) % 4
                    pb = arena[:, OFF_PB + ((2 * pr) % 4) * 512: OFF_PB + ((2 * pr) % 4) * 512 + 1024]
                    blk = (2 * pr) // 4
                    if blk == 0:
                        inst = sc.activation(pb, psum[:, sbk * 512: sbk * 512 + 1024], AF.Exp,
                                             bias=negc[:, cslot:cslot + 1], scale=ISQ)
                    else:
                        ci = blk - 1
                        col = cslot * 8 + ci
                        scol = ci if qb == 0 else 3 + ci
                        inst = sc.activation(pb, psum[:, sbk * 512: sbk * 512 + 1024], AF.Exp,
                                             bias=btab[:, col:col + 1],
                                             scale=scaletab[:, scol:scol + 1])
                    inst.then_inc(g["s_exp"].h, 1)
            PS.append(s_op)
            g["s_exp"].bump(npair)

            kslot_free[kslot] = (g["s_sb"].h, g["s_sb"].n)
            vslot_free[vslot] = (g["s_pf"].h, g["s_pf"].n)

            pf_all = g["s_pf"].n
            islot = h % 2

            def v_inv(v, pf_all=pf_all, islot=islot):
                v.wait_ge(g["s_pf"].h, pf_all)
                v.reciprocal(invsc[0:1, islot * 512:(islot + 1) * 512], psum[0:1, 6 * 512:6 * 512 + 512]).then_inc(g["s_inv"].h, 1)
            PV.append(v_inv)
            g["s_inv"].bump()
            for k in range(4):
                set_bank(k, g["s_exp"], g["s_exp"].n)
            return (h, islot, zbank, g["s_inv"].n)

        def emit_B_tail(binfo):
            h, islot, zbank, inv_tgt = binfo
            zev_prev = g["s_zev"].n

            def t_invb(t, inv_tgt=inv_tgt, islot=islot, zev_prev=zev_prev):
                t.wait_ge(g["s_inv"].h, inv_tgt)
                t.wait_ge(g["s_zev"].h, zev_prev)
                inst = t.matmul(BANK(7), onesrow[:, :], invsc[0:1, islot * 512:(islot + 1) * 512],
                                start=True, stop=True)
                inst.then_inc(g["s_invb"].h, 1)
            PT.append(t_invb)
            g["s_invb"].bump()
            invb_tgt = g["s_invb"].n
            zgate = zgate_holder[0]
            zgate_holder[0] = None

            def v_z(v, h=h, invb_tgt=invb_tgt, zbank=zbank, zgate=zgate):
                v.wait_ge(g["s_invb"].h, invb_tgt)
                if zgate is not None:
                    v.wait_ge(zgate[0], zgate[1])
                v.tensor_copy(sgscr[:, 0:512], BANK(7))
                v.drain()
                v.tensor_tensor(arena[:, OFF_Z + h * 512: OFF_Z + (h + 1) * 512],
                                BANK(zbank), sgscr[:, 0:512], ALU.mult).then_inc(g["s_zev"].h, 1)
            PV.append(v_z)
            g["s_zev"].bump()
            set_bank(zbank, g["s_zev"], g["s_zev"].n)

        zgate_holder = [None]

        def emit_oproj(qb):
            zev_all = g["s_zev"].n
            for dt_i in range(16):
                co = load_wchunk(wo_s[dt_i, :, :], 2048)
                ob = dt_i % 4
                o_base = g["s_o"].n
                bwo = bank_wait_t(ob)

                def t_o(t, co=co, dt_i=dt_i, ob=ob, zev_all=zev_all, bwo=bwo):
                    t.wait_ge(co[1], co[2])
                    if dt_i == 0:
                        t.wait_ge(g["s_zev"].h, zev_all)
                    for (sh, n) in bwo:
                        t.wait_ge(sh, n)
                    inst = None
                    for hc in range(16):
                        inst = t.matmul(BANK(ob), wap(co, hc * 128, (hc + 1) * 128),
                                        arena[:, OFF_Z + hc * 512: OFF_Z + (hc + 1) * 512],
                                        start=(hc == 0), stop=(hc == 15))
                    inst.then_inc(g["s_o"].h, 1)
                PT.append(t_o)
                g["s_o"].bump()
                chunk_free_at(co, g["s_o"], o_base + 1)

                def v_o(v, dt_i=dt_i, ob=ob, o_base=o_base, qb=qb):
                    v.wait_ge(g["s_o"].h, o_base + 1)
                    xs = x[:, dt_i * T + qb * TH: dt_i * T + qb * TH + TH]
                    v.tensor_tensor(xs, xs, BANK(ob), ALU.add).then_inc(g["s_of"].h, 1)
                PV.append(v_o)
                g["s_of"].bump()
                set_bank(ob, g["s_of"], g["s_of"].n)

        for qb in (0, 1):
            if qb == 1:
                zgate_holder[0] = (g["s_o"].h, g["s_o"].n)
            kinfos, vinfos, cinfos, binfos = {}, {}, {}, {}
            for i in range(18):
                if i < 16:
                    kinfos[i] = emit_kload(i, qb)
                    vinfos[i] = emit_vload(i, qb)
                    cinfos[i] = emit_c(i, qb)
                if 1 <= i <= 16:
                    hh = i - 1
                    binfos[hh] = emit_B(hh, qb, kinfos[hh], vinfos[hh], cinfos[hh])
                if 2 <= i <= 17:
                    emit_B_tail(binfos[i - 2])
            emit_oproj(qb)
        set_bank(6, g["s_inv"], g["s_inv"].n)
        set_bank(7, g["s_zev"], g["s_zev"].n)

    lnx1 = ph_ln([(g["s_x"].h, g["s_x"].n), (g["s_const"].h, g["s_const"].n)])
    ph_router(lnx1)
    for e_idx in range(9):
        ph_expert(e_idx, 0)

    if ATTN_ON:
        zf1 = g["s_zf"].n
        lnx2 = ph_ln([(g["s_zf"].h, zf1)])
        qe_all = ph_qkv(lnx2)
        ph_attn(qe_all)

    if FFN2_ON:
        of_all = g["s_of"].n
        gate3 = [(g["s_of"].h, of_all)] if ATTN_ON else [(g["s_zf"].h, g["s_zf"].n)]
        lnx3 = ph_ln(gate3)
        ph_router(lnx3)
        for e_idx in range(9):
            ph_expert(e_idx, 1)

    zf_total = g["s_zf"].n
    of_total = g["s_of"].n

    out_tgt = g["s_out"].n + 16 * NDC

    def y_final(y, zf_total=zf_total, of_total=of_total, out_tgt=out_tgt):
        y.wait_ge(g["s_zf"].h, zf_total)
        if ATTN_ON and not FFN2_ON:
            y.wait_ge(g["s_of"].h, of_total)
        for dc in range(NDC):
            y.dma_start(out=y_out[dc, :, :], in_=x[:, dc * T:(dc + 1) * T]).then_inc(g["s_out"].h, 16)
        y.wait_ge(g["s_out"].h, out_tgt)
    PY.append(y_final)
    g["s_out"].bump(16 * NDC)

    with nc.Block() as block:
        @block.tensor
        def _(t):
            for f in PT:
                f(t)

        @block.vector
        def _(v):
            for f in PV:
                f(v)

        @block.scalar
        def _(sc):
            for f in PS:
                f(sc)

        @block.sync
        def _(y):
            for f in PY:
                f(y)

        @block.gpsimd
        def _(gp):
            for f in PG:
                f(gp)

    stack.close()
    return nc


def up_stream(WT, nft):
    A = WT.reshape(16, 128, nft, 128)
    return np.ascontiguousarray(A.transpose(2, 1, 0, 3).reshape(nft, 128, 2048))


def down_stream(WdT):
    nfc = WdT.shape[0] // 128
    A = WdT.reshape(nfc, 128, 16, 128)
    return np.ascontiguousarray(A.transpose(2, 1, 0, 3).reshape(16, 128, nfc * 128))


def wv_stream(WvT):
    A = WvT.reshape(4, 4, 128, 4, 512)
    return np.ascontiguousarray(A.transpose(3, 0, 2, 1, 4).reshape(4, 4, 128, 2048))


def host_prepare(inputs):
    f32 = np.float32
    gw = lambda k: np.asarray(inputs[k], f32)
    Wq, Wk, Wv, Wo = gw("L0_attn_Wq"), gw("L0_attn_Wk"), gw("L0_attn_Wv"), gw("L0_attn_Wo")
    Wr, Wg, Wu, Wd = gw("L0_ffn_Wr"), gw("L0_ffn_Wg"), gw("L0_ffn_Wu"), gw("L0_ffn_Wd")
    Wsg, Wsu, Wsd = gw("L0_ffn_Wsg"), gw("L0_ffn_Wsu"), gw("L0_ffn_Wsd")
    x = gw("L0_attn_x")

    shared_w = {
        "wr_in": np.ascontiguousarray(Wr.T.reshape(16, 128, 8).transpose(1, 0, 2).reshape(128, 128)),
        "wsg_s": up_stream(np.ascontiguousarray(Wsg.T), NFT_S),
        "wsu_s": up_stream(np.ascontiguousarray(Wsu.T), NFT_S),
        "wg_s": np.stack([up_stream(np.ascontiguousarray(Wg[e].T), NFT_E) for e in range(NE)]),
        "wu_s": np.stack([up_stream(np.ascontiguousarray(Wu[e].T), NFT_E) for e in range(NE)]),
        "wsd_s": down_stream(np.ascontiguousarray(Wsd.T)),
        "wd_s": np.stack([down_stream(np.ascontiguousarray(Wd[e].T)) for e in range(NE)]),
        "wk_s": up_stream(np.ascontiguousarray(Wk.T), 16),
        "wv_s": wv_stream(np.ascontiguousarray(Wv.T)),
        "wq_s": up_stream(np.ascontiguousarray(Wq.T), 16),
        "wo_s": up_stream(np.ascontiguousarray(Wo.T), 16),
    }
    p_idx = np.arange(128)[:, None]
    s_idx = np.arange(896)[None, :]
    consts = {
        "m0_in": np.where(s_idx < p_idx + 384, BIGNEG, 0.0).astype(BF),
        "identf_in": np.eye(128, dtype=f32),
        "identb_in": np.eye(128).astype(BF),
        "c2048_in": np.full((128, 1), 1.0 / 2048.0).astype(BF),
        "onescol_in": np.ones((128, 1), BF),
        "onesrow_in": np.ones((1, 128), f32),
        "negisq_in": np.full((1, 128), -ISQ, f32),
        "fconst_in": np.ascontiguousarray(
            np.stack([np.full(128, EPS, f32), np.full(128, ISQ, f32),
                      np.full(128, 1.21, f32)], axis=1)),
    }
    in_maps = []
    for c in range(8):
        b, j = c // 4, c % 4
        blkA, blkB = j, 7 - j
        xl = np.concatenate([x[b, blkA * 512:(blkA + 1) * 512],
                             x[b, blkB * 512:(blkB + 1) * 512]], axis=0)
        x_fm = np.ascontiguousarray(xl.T.reshape(16, 128, T))
        slots_a = [0, 1, 2]
        slots_b = [0, 1, 2, 3, 6, 5, 4]
        cnt = np.asarray([1.0 if gg < blkA else 0.0 for gg in slots_a] +
                         [1.0 if gg < blkB else 0.0 for gg in slots_b], f32)
        im = {
            "x_in": x_fm,
            "sumind_in": np.ascontiguousarray(np.broadcast_to(cnt, (128, 10))).astype(BF),
            "scaletab_in": np.ascontiguousarray(np.broadcast_to(np.where(cnt > 0, ISQ, 0.0).astype(f32), (128, 10))),
            "padbias_in": np.ascontiguousarray(np.broadcast_to(np.where(cnt > 0, 0.0, PADB).astype(f32), (128, 10))),
        }
        im.update(shared_w)
        im.update(consts)
        in_maps.append(im)
    return in_maps


def host_gather(results):
    out = np.empty((2, 4096, 2048), np.float32)
    for c in range(8):
        b, j = c // 4, c % 4
        y = np.asarray(results[c]["y_out"], np.float32)
        xl = y.reshape(2048, T).T
        out[b, j * 512:(j + 1) * 512] = xl[0:512]
        out[b, (7 - j) * 512:(8 - j) * 512] = xl[512:1024]
    return out


def kernel(**inputs):
    in_maps = host_prepare(inputs)
    nc = build_program()
    res = run_bass_kernel_spmd(nc, in_maps, core_ids=list(range(8)))
    return host_gather(res.results)


# version 09 (final)
# speedup vs baseline: 1.0352x; 1.0352x over previous
import numpy as np
import ml_dtypes
import concourse.bass as bass
import concourse.mybir as mybir
from concourse.bass_utils import run_bass_kernel_spmd

BF16 = mybir.dt.bfloat16
F32 = mybir.dt.float32
AF = mybir.ActivationFunctionType
ALU = mybir.AluOpType
AX = mybir.AxisListType.X
BF = ml_dtypes.bfloat16

D, NDC, T, TH, NTH = 2048, 16, 1024, 512, 2
FS, FE, NE = 2816, 1408, 8
NFT_S, NFT_E = 22, 11
NH = 16
EPS = 1e-5
ISQ = float(1.0 / np.sqrt(128.0))
BIGNEG = -3.0e30
PADB = -100.0
WSLOT = 4
WSLOT_W = 2048

OFF_LNX = 0
OFF_HB = 16384
OFF_VSB = 16384
OFF_KST = 32768
OFF_QT = 34816
OFF_VB = 0
OFF_PB = 8192
OFF_Z = 10240
OFF_KB = 19456
OFF_LNSCR = 33280
OFF_NSCR = 33280
ARENA_N = 51200
KOFF = 0
VOFF = 2097152
KVN = 4194304
TOFF = 4194304
KVN2 = 4194320

ATTN_ON = True
FFN2_ON = True


def build_program():
    nc = bass.Bass("TRN2", target_bir_lowering=False)
    dt = nc.dram_tensor
    x_in = dt("x_in", [NDC, 128, T], F32, kind="ExternalInput").ap()
    wr_in = dt("wr_in", [128, NDC * 8], F32, kind="ExternalInput").ap()
    wsg_s = dt("wsg_s", [NFT_S, 128, 2048], F32, kind="ExternalInput").ap()
    wsu_s = dt("wsu_s", [NFT_S, 128, 2048], F32, kind="ExternalInput").ap()
    wg_s = dt("wg_s", [NE, NFT_E, 128, 2048], F32, kind="ExternalInput").ap()
    wu_s = dt("wu_s", [NE, NFT_E, 128, 2048], F32, kind="ExternalInput").ap()
    wsd_s = dt("wsd_s", [16, 128, FS], F32, kind="ExternalInput").ap()
    wd_s = dt("wd_s", [NE, 16, 128, FE], F32, kind="ExternalInput").ap()
    wk_s = dt("wk_s", [16, 128, 2048], F32, kind="ExternalInput").ap()
    wv_s = dt("wv_s", [4, 4, 128, 2048], F32, kind="ExternalInput").ap()
    wq_s = dt("wq_s", [16, 128, 2048], F32, kind="ExternalInput").ap()
    wo_s = dt("wo_s", [16, 128, 2048], F32, kind="ExternalInput").ap()
    m0_in = dt("m0_in", [128, 896], BF16, kind="ExternalInput").ap()
    identf_in = dt("identf_in", [128, 128], F32, kind="ExternalInput").ap()
    identb_in = dt("identb_in", [128, 128], BF16, kind="ExternalInput").ap()
    c2048_in = dt("c2048_in", [128, 1], BF16, kind="ExternalInput").ap()
    onescol_in = dt("onescol_in", [128, 1], BF16, kind="ExternalInput").ap()
    onesrow_in = dt("onesrow_in", [1, 128], F32, kind="ExternalInput").ap()
    negisq_in = dt("negisq_in", [1, 128], F32, kind="ExternalInput").ap()
    sumind_in = dt("sumind_in", [128, 10], BF16, kind="ExternalInput").ap()
    scaletab_in = dt("scaletab_in", [128, 10], F32, kind="ExternalInput").ap()
    padbias_in = dt("padbias_in", [128, 10], F32, kind="ExternalInput").ap()
    fconst_in = dt("fconst_in", [128, 3], F32, kind="ExternalInput").ap()
    y_out = dt("y_out", [NDC, 128, T], F32, kind="ExternalOutput").ap()
    k_send = dt("k_send", [2097168], BF16).ap()
    k_gath = dt("k_gath", [4, 2097168], BF16).ap()
    v_send = dt("v_send", [2097152], BF16).ap()
    v_gath = dt("v_gath", [4, 2097152], BF16).ap()

    from contextlib import ExitStack
    stack = ExitStack()
    sbt = lambda name, shape, dtyp: stack.enter_context(nc.sbuf_tensor(name, shape, dtyp))

    x = sbt("x", [128, NDC * T], F32)
    arena = sbt("arena", [128, ARENA_N], BF16)
    wring = sbt("wring", [128, WSLOT * WSLOT_W], BF16)
    sgscr = sbt("sgscr", [128, 4 * TH], BF16)
    uwscr = sbt("uwscr", [128, TH], F32)
    wbscr = sbt("wbscr", [128, T], F32)
    rowpool = sbt("rowpool", [1, 2048], F32)
    negc = sbt("negc", [128, 2], F32)
    btab = sbt("btab", [128, 16], F32)
    wsb = sbt("wsb", [64, 128], F32)
    wtok = sbt("wtok", [128, 64], F32)
    rscr = sbt("rscr", [128, 48], F32)
    m0 = sbt("m0", [128, 896], BF16)
    identf = sbt("identf", [128, 128], F32)
    identb = sbt("identb", [128, 128], BF16)
    c2048 = sbt("c2048", [128, 1], BF16)
    onescol = sbt("onescol", [128, 1], BF16)
    onesrow = sbt("onesrow", [1, 128], F32)
    negisq = sbt("negisq", [1, 128], F32)
    sumind = sbt("sumind", [128, 10], BF16)
    scaletab = sbt("scaletab", [128, 10], F32)
    padbias = sbt("padbias", [128, 10], F32)
    wrb = sbt("wrb", [128, 128], BF16)
    fconst = sbt("fconst", [128, 3], F32)
    nc.const_aps.register(F32, EPS, fconst[:, 0:1])
    nc.const_aps.register(F32, ISQ, fconst[:, 1:2])
    nc.const_aps.register(F32, 1.21, fconst[:, 2:3])
    qn_sb = sbt("qn_sb", [1, 32], F32)
    kq_sb = sbt("kq_sb", [1, 16], BF16)
    knb_sb = sbt("knb_sb", [1, 64], BF16)
    kmx = sbt("kmx", [1, 2], F32)
    nscr = sbt("nscr", [128, 1024], BF16)
    cqs = sbt("cqs", [1, 2], F32)
    cscs = sbt("cscs", [1, 2], F32)
    crow = sbt("crow", [1, 128], F32)
    invsc = rowpool

    psum = stack.enter_context(nc.psum_tensor("psum", [128, 4096], F32))
    BANK = lambda k: psum[:, k * 512:(k + 1) * 512]

    class Sem:
        __slots__ = ("h", "n")

        def __init__(self, name):
            self.h = stack.enter_context(nc.semaphore(name))
            self.n = 0

        def bump(self, k=1):
            self.n += k
            return self.n

    nm = ("s_const s_x s_wfree s_cast s_stat s_lnrow s_lnb s_lnx s_rt s_rw s_rwt "
          "s_rwe s_rext s_wb s_wbe s_fft s_sg s_h s_hfree s_z s_zf "
          "s_kproj s_ke s_ksend s_vproj s_ve s_vsend s_qproj s_qe s_cc "
          "s_kfree s_vfree s_sa s_saf s_cb s_sb s_exp s_pf s_inv s_invb "
          "s_zev s_ofree s_o s_of s_out s_wrb s_lnsq s_kn1 s_kn2 s_knl s_qn1 s_qn2 s_cq s_csc").split()
    g = {n: Sem(n) for n in nm}
    s_wq = [Sem(f"s_wq{i}") for i in range(WSLOT)]
    s_kl = [Sem(f"s_kl{i}") for i in range(3)]
    s_ks = [Sem(f"s_ks{i}") for i in range(2)]
    s_vl = [Sem(f"s_vl{i}") for i in range(2)]

    PT, PV, PS, PY, PG = [], [], [], [], []

    bank_free = [None] * 8

    def bank_wait_t(k):
        w = []
        if bank_free[k] is not None:
            sm, n = bank_free[k]
            w.append((sm.h, n))
            bank_free[k] = None
        return w

    def set_bank(k, sm, n):
        bank_free[k] = (sm, n)

    wi = [0]
    ring_free = [None] * WSLOT

    def load_wchunk(src_ap, width):
        i = wi[0]
        wi[0] += 1
        slot = i % WSLOT
        sw = s_wq[slot]
        tgt = sw.bump(16)
        fw = ring_free[slot]
        ring_free[slot] = "PENDING"
        assert fw != "PENDING", f"chunk slot {slot} reloaded before free event set"

        def g_op(gp, fw=fw, slot=slot, src_ap=src_ap, width=width, sw=sw):
            if fw is not None:
                gp.wait_ge(fw[0], fw[1])
            gp.dma_start(out=wring[:, slot * WSLOT_W: slot * WSLOT_W + width],
                         in_=src_ap).then_inc(sw.h, 16)
        PG.append(g_op)
        return (slot, sw.h, tgt)

    def chunk_free_at(ch, sem, count):
        slot = ch[0]
        assert ring_free[slot] == "PENDING"
        ring_free[slot] = (sem.h, count)

    def wap(ch, a, b):
        slot, _, _ = ch
        return wring[:, slot * WSLOT_W + a: slot * WSLOT_W + b]

    def y_load(y):
        for (dst, src) in [(m0, m0_in), (identf, identf_in), (identb, identb_in),
                           (c2048, c2048_in), (onescol, onescol_in),
                           (onesrow, onesrow_in), (negisq, negisq_in),
                           (sumind, sumind_in), (scaletab, scaletab_in),
                           (padbias, padbias_in), (fconst, fconst_in)]:
            y.dma_start(out=dst[:, :], in_=src[:, :]).then_inc(g["s_const"].h, 16)
        for dc in range(NDC):
            y.dma_start(out=x[:, dc * T:(dc + 1) * T], in_=x_in[dc, :, :]).then_inc(g["s_x"].h, 16)
    PY.append(y_load)
    g["s_const"].bump(16 * 11)
    g["s_x"].bump(16 * NDC)

    def g_load(gp):
        gp.dma_start(out=wrb[:, :], in_=wr_in[:, :]).then_inc(g["s_wrb"].h, 16)
    PG.append(g_load)
    g["s_wrb"].bump(16)

    def ph_ln(gate_waits, gate_by_dc=None):
        ab = lambda sl: arena[:, OFF_LNSCR + sl * 1024: OFF_LNSCR + (sl + 1) * 1024]
        sq = lambda sl: arena[:, OFF_LNSCR + 2048 + sl * 1024: OFF_LNSCR + 2048 + (sl + 1) * 1024]
        cast_base = g["s_cast"].n
        stat_base = g["s_stat"].n

        def s_cast_op(sc, cast_base=cast_base, stat_base=stat_base,
                      gate_waits=list(gate_waits), gate_by_dc=gate_by_dc):
            for dc in range(NDC):
                if dc == 0:
                    for (sh, n) in gate_waits:
                        sc.wait_ge(sh, n)
                if gate_by_dc is not None:
                    sc.wait_ge(gate_by_dc[0], gate_by_dc[1][dc])
                if dc >= 2:
                    sc.wait_ge(g["s_stat"].h, stat_base + (dc - 1) * 2)
                sc.activation(ab(dc % 2), x[:, dc * T:(dc + 1) * T], AF.Copy)
                sc.activation(sq(dc % 2), x[:, dc * T:(dc + 1) * T], AF.Square).then_inc(g["s_cast"].h, 1)
        PS.append(s_cast_op)
        g["s_cast"].bump(NDC)

        bw0 = []
        for k in range(4):
            bw0 += bank_wait_t(k)

        def t_stat(t, cast_base=cast_base, bw0=bw0):
            for (sh, n) in bw0:
                t.wait_ge(sh, n)
            for dc in range(NDC):
                t.wait_ge(g["s_cast"].h, cast_base + dc + 1)
                for th in range(NTH):
                    t.matmul(psum[0:1, (2 * th) * 512:(2 * th) * 512 + 512],
                             c2048[:, :], ab(dc % 2)[:, th * TH:(th + 1) * TH],
                             start=(dc == 0), stop=(dc == NDC - 1))
                    inst = t.matmul(psum[0:1, (2 * th + 1) * 512:(2 * th + 1) * 512 + 512],
                                    c2048[:, :], sq(dc % 2)[:, th * TH:(th + 1) * TH],
                                    start=(dc == 0), stop=(dc == NDC - 1))
                    if th == 1:
                        inst.then_inc(g["s_stat"].h, 2)
        PT.append(t_stat)
        g["s_stat"].bump(2 * NDC)
        stat_all = g["s_stat"].n

        row_base = g["s_lnrow"].n

        def v_chain1(v, stat_all=stat_all):
            v.wait_ge(g["s_stat"].h, stat_all)
            for th in range(NTH):
                mean_r = psum[0:1, (2 * th) * 512:(2 * th) * 512 + 512]
                msq_r = psum[0:1, (2 * th + 1) * 512:(2 * th + 1) * 512 + 512]
                mu_sb = rowpool[0:1, th * 512:(th + 1) * 512]
                scr = rowpool[0:1, 1024 + th * 512:1024 + (th + 1) * 512]
                v.tensor_copy(mu_sb, mean_r)
                v.drain()
                v.tensor_tensor(scr, mu_sb, mu_sb, ALU.mult)
                v.drain()
                v.tensor_tensor(msq_r, msq_r, scr, ALU.subtract).then_inc(g["s_lnrow"].h, 1)
        PV.append(v_chain1)
        g["s_lnrow"].bump(NTH)

        sq_base = g["s_lnsq"].n

        def s_sqrt(sc, row_base=row_base):
            for th in range(NTH):
                sc.wait_ge(g["s_lnrow"].h, row_base + th + 1)
                msq_r = psum[0:1, (2 * th + 1) * 512:(2 * th + 1) * 512 + 512]
                sc.activation(msq_r, msq_r, AF.Sqrt, bias=EPS).then_inc(g["s_lnsq"].h, 1)
        PS.append(s_sqrt)
        g["s_lnsq"].bump(NTH)
        chain2 = g["s_lnrow"].n

        def v_recip(v, sq_base=sq_base):
            for th in range(NTH):
                v.wait_ge(g["s_lnsq"].h, sq_base + th + 1)
                msq_r = psum[0:1, (2 * th + 1) * 512:(2 * th + 1) * 512 + 512]
                v.reciprocal(rowpool[0:1, 1024 + th * 512: 1024 + (th + 1) * 512], msq_r).then_inc(g["s_lnrow"].h, 1)
        PV.append(v_recip)
        g["s_lnrow"].bump(NTH)

        lnb_base = g["s_lnb"].n
        bw2 = []
        for k in range(4, 8):
            bw2 += bank_wait_t(k)

        def t_bcast(t, chain2=chain2, bw2=bw2):
            for (sh, n) in bw2:
                t.wait_ge(sh, n)
            for th in range(NTH):
                t.wait_ge(g["s_lnrow"].h, chain2 + th + 1)
                mu_sl = rowpool[0:1, th * 512:(th + 1) * 512]
                inv_sl = rowpool[0:1, 1024 + th * 512:1024 + (th + 1) * 512]
                t.matmul(BANK(4 + th * 2), onesrow[:, :], mu_sl, start=True, stop=True)
                inst = t.matmul(BANK(5 + th * 2), onesrow[:, :], inv_sl, start=True, stop=True)
                inst.then_inc(g["s_lnb"].h, 1)
        PT.append(t_bcast)
        g["s_lnb"].bump(NTH)

        lnx_base0 = g["s_lnx"].n

        def v_apply(v, lnb_base=lnb_base):
            for th in range(NTH):
                v.wait_ge(g["s_lnb"].h, lnb_base + th + 1)
                for dc in range(NDC):
                    lx = arena[:, OFF_LNX + dc * T + th * TH: OFF_LNX + dc * T + (th + 1) * TH]
                    v.tensor_tensor(lx, x[:, dc * T + th * TH: dc * T + (th + 1) * TH],
                                    BANK(4 + th * 2), ALU.subtract)
                    v.drain()
                    v.tensor_tensor(lx, lx, BANK(5 + th * 2), ALU.mult).then_inc(g["s_lnx"].h, 1)
        PV.append(v_apply)
        g["s_lnx"].bump(NDC * NTH)
        for k in range(8):
            set_bank(k, g["s_lnx"], g["s_lnx"].n)
        return (lnx_base0 + NDC, lnx_base0 + 2 * NDC)

    def ph_router(lnx_all):
        rt_base = g["s_rt"].n
        bw6 = bank_wait_t(6)

        def t_logits(t, lnx_all=lnx_all, bw6=bw6):
            for (sh, n) in bw6:
                t.wait_ge(sh, n)
            t.wait_ge(g["s_wrb"].h, 16)
            for tq in range(8):
                if tq == 0:
                    t.wait_ge(g["s_lnx"].h, lnx_all[0])
                if tq == 4:
                    t.wait_ge(g["s_lnx"].h, lnx_all[1])
                pr = psum[:, 6 * 512 + tq * 8: 6 * 512 + (tq + 1) * 8]
                inst = None
                for dc in range(NDC):
                    inst = t.matmul(pr, arena[:, OFF_LNX + dc * T + tq * 128: OFF_LNX + dc * T + (tq + 1) * 128],
                                    wrb[:, dc * 8:(dc + 1) * 8],
                                    start=(dc == 0), stop=(dc == NDC - 1))
                inst.then_inc(g["s_rt"].h, 1)
        PT.append(t_logits)
        g["s_rt"].bump(8)

        rw_base = g["s_rw"].n
        W = wbscr

        def v_rs1(v, rt_base=rt_base):
            v.wait_ge(g["s_rt"].h, rt_base + 8)
            for tq in range(8):
                pr = psum[:, 6 * 512 + tq * 8: 6 * 512 + (tq + 1) * 8]
                v.tensor_reduce(W[:, tq:tq + 1], pr, AX, ALU.max, negate=True)
            v.drain()
            v.engine_nop().then_inc(g["s_rw"].h, 1)
        PV.append(v_rs1)
        g["s_rw"].bump()

        def s_rexp(sc, rw_base=rw_base):
            sc.wait_ge(g["s_rw"].h, rw_base + 1)
            inst = None
            for tq in range(8):
                pr = psum[:, 6 * 512 + tq * 8: 6 * 512 + (tq + 1) * 8]
                inst = sc.activation(W[:, 64 + tq * 8:64 + (tq + 1) * 8], pr, AF.Exp,
                                     bias=W[:, tq:tq + 1])
            inst.then_inc(g["s_rw"].h, 1)
        PS.append(s_rexp)
        g["s_rw"].bump()

        def v_rs2(v, rw_base=rw_base):
            v.wait_ge(g["s_rw"].h, rw_base + 2)
            for tq in range(8):
                v.reduce_sum(W[:, 24 + tq:25 + tq], W[:, 64 + tq * 8:64 + (tq + 1) * 8], AX)
            v.drain()
            v.reciprocal(W[:, 32:40], W[:, 24:32])
            v.drain()
            for tq in range(8):
                v.tensor_scalar(W[:, 128 + tq * 8:128 + (tq + 1) * 8],
                                W[:, 64 + tq * 8:64 + (tq + 1) * 8],
                                W[:, 32 + tq:33 + tq], None, ALU.mult)
            v.drain()
            for tq in range(8):
                v.reduce_max(W[:, 8 + tq:9 + tq], W[:, 128 + tq * 8:128 + (tq + 1) * 8], AX)
            v.drain()
            for tq in range(8):
                v.tensor_scalar(W[:, 192 + tq * 8:192 + (tq + 1) * 8],
                                W[:, 128 + tq * 8:128 + (tq + 1) * 8],
                                W[:, 8 + tq:9 + tq], None, ALU.is_equal)
            v.drain()
            for tq in range(8):
                v.scalar_tensor_tensor(W[:, 192 + tq * 8:192 + (tq + 1) * 8],
                                       W[:, 192 + tq * 8:192 + (tq + 1) * 8], -1.0e30,
                                       W[:, 128 + tq * 8:128 + (tq + 1) * 8], ALU.mult, ALU.add)
            v.drain()
            for tq in range(8):
                v.reduce_max(W[:, 16 + tq:17 + tq], W[:, 192 + tq * 8:192 + (tq + 1) * 8], AX)
            v.drain()
            for tq in range(8):
                v.tensor_scalar(W[:, 256 + tq * 8:256 + (tq + 1) * 8],
                                W[:, 128 + tq * 8:128 + (tq + 1) * 8],
                                W[:, 16 + tq:17 + tq], None, ALU.is_ge)
            v.drain()
            for tq in range(8):
                v.tensor_tensor(wtok[:, tq:tq + 57:8], W[:, 128 + tq * 8:128 + (tq + 1) * 8],
                                W[:, 256 + tq * 8:256 + (tq + 1) * 8], ALU.mult)
            v.drain()
            v.engine_nop().then_inc(g["s_rw"].h, 1)
        PV.append(v_rs2)
        g["s_rw"].bump()
        rw_all = g["s_rw"].n

        bw7 = bank_wait_t(7)

        def t_wtr(t, rw_all=rw_all, bw7=bw7):
            t.wait_ge(g["s_rw"].h, rw_all)
            for (sh, n) in bw7:
                t.wait_ge(sh, n)
            inst = t.transpose(psum[0:64, 7 * 512:7 * 512 + 128], wtok[:, :], identf[:, :])
            inst.then_inc(g["s_rwt"].h, 1)
        PT.append(t_wtr)
        g["s_rwt"].bump()
        rwt_all = g["s_rwt"].n

        def v_wev(v, rwt_all=rwt_all):
            v.wait_ge(g["s_rwt"].h, rwt_all)
            v.tensor_copy(wsb[:, :], psum[0:64, 7 * 512:7 * 512 + 128]).then_inc(g["s_rwe"].h, 1)
        PV.append(v_wev)
        g["s_rwe"].bump()
        set_bank(6, g["s_rw"], rw_all)
        set_bank(7, g["s_rwe"], g["s_rwe"].n)

    ftctr = [0]
    zctr = [0]
    zf_by_dt = {}
    of_by_dt = {}

    def ph_expert(e_idx, pass_i):
        shared = (e_idx == 0)
        nft = NFT_S if shared else NFT_E
        if shared:
            gsrc = lambda ft: wsg_s[ft, :, :]
            usrc = lambda ft: wsu_s[ft, :, :]
            dsrc = lambda dt_i, half: wsd_s[dt_i, :, half * 1408:(half + 1) * 1408]
        else:
            e = e_idx - 1
            gsrc = lambda ft, e=e: wg_s[e, ft, :, :]
            usrc = lambda ft, e=e: wu_s[e, ft, :, :]
            dsrc = lambda dt_i, half, e=e: wd_s[e, dt_i, :, :]

        hgate = g["s_z"].n
        if not shared:
            e = e_idx - 1
            slot = e % 2
            rwe_tgt = g["s_rwe"].n
            wb_need = g["s_wb"].n - 2 if e >= 2 else None

            rext_pre = g["s_rext"].n

            def y_ext(y, slot=slot, e=e, rwe_tgt=rwe_tgt, wb_need=wb_need, rext_pre=rext_pre):
                y.wait_ge(g["s_rwe"].h, rwe_tgt)
                if rext_pre > 0:
                    y.wait_ge(g["s_rext"].h, rext_pre)
                if wb_need is not None:
                    y.wait_ge(g["s_wb"].h, wb_need + 2)
                y.dma_start(out=rowpool[0:1, slot * 1024:(slot + 1) * 1024],
                            in_=wsb[e * 8:(e + 1) * 8, :]).then_inc(g["s_rext"].h, 16)
            PY.append(y_ext)
            g["s_rext"].bump(16)
            rext_tgt = g["s_rext"].n
            wb_base = g["s_wb"].n
            bw6a = bank_wait_t(6)
            wbe_pre = g["s_wbe"].n

            def t_wb(t, slot=slot, rext_tgt=rext_tgt, bw6a=bw6a, wbe_pre=wbe_pre):
                t.wait_ge(g["s_rext"].h, rext_tgt)
                for (sh, n) in bw6a:
                    t.wait_ge(sh, n)
                for th in range(NTH):
                    if th == 1:
                        t.wait_ge(g["s_wbe"].h, wbe_pre + 1)
                    inst = t.matmul(BANK(6), onesrow[:, :],
                                    rowpool[0:1, slot * 1024 + th * 512: slot * 1024 + (th + 1) * 512],
                                    start=True, stop=True)
                    inst.then_inc(g["s_wb"].h, 1)
            PT.append(t_wb)
            g["s_wb"].bump(NTH)

            def v_wb(v, wb_base=wb_base):
                for th in range(NTH):
                    v.wait_ge(g["s_wb"].h, wb_base + th + 1)
                    v.tensor_copy(wbscr[:, th * 512:(th + 1) * 512], BANK(6)).then_inc(g["s_wbe"].h, 1)
            PV.append(v_wb)
            g["s_wbe"].bump(NTH)
            set_bank(6, g["s_wbe"], g["s_wbe"].n)
        wbe_tgt = g["s_wbe"].n

        for ft in range(nft):
            cg = load_wchunk(gsrc(ft), 2048)
            cu = load_wchunk(usrc(ft), 2048)
            B0 = (ftctr[0] % 2) * 4
            ftctr[0] += 1
            fft_base = g["s_fft"].n
            bws = [bank_wait_t(k) for k in range(B0, B0 + 4)]

            def t_up(t, cg=cg, cu=cu, B0=B0, bws=bws):
                t.wait_ge(cg[1], cg[2])
                for bw in bws:
                    for (sh, n) in bw:
                        t.wait_ge(sh, n)
                inst = None
                for dc in range(NDC):
                    for th in range(NTH):
                        inst = t.matmul(BANK(B0 + th), wap(cg, dc * 128, (dc + 1) * 128),
                                        arena[:, OFF_LNX + dc * T + th * TH: OFF_LNX + dc * T + (th + 1) * TH],
                                        start=(dc == 0), stop=(dc == NDC - 1))
                inst.then_inc(g["s_fft"].h, 1)
                t.wait_ge(cu[1], cu[2])
                for dc in range(NDC):
                    for th in range(NTH):
                        inst = t.matmul(BANK(B0 + 2 + th), wap(cu, dc * 128, (dc + 1) * 128),
                                        arena[:, OFF_LNX + dc * T + th * TH: OFF_LNX + dc * T + (th + 1) * TH],
                                        start=(dc == 0), stop=(dc == NDC - 1))
                inst.then_inc(g["s_fft"].h, 1)
            PT.append(t_up)
            g["s_fft"].bump(2)
            chunk_free_at(cg, g["s_fft"], fft_base + 1)
            chunk_free_at(cu, g["s_fft"], fft_base + 2)
            fslot = ft % 2
            sg_base = g["s_sg"].n

            def s_silu(sc, B0=B0, fft_base=fft_base, fslot=fslot):
                sc.wait_ge(g["s_fft"].h, fft_base + 1)
                for th in range(NTH):
                    sc.activation(sgscr[:, (fslot * 2 + th) * TH:(fslot * 2 + th + 1) * TH],
                                  BANK(B0 + th), AF.Silu).then_inc(g["s_sg"].h, 1)
            PS.append(s_silu)
            g["s_sg"].bump(NTH)
            sg_now = g["s_sg"].n
            h_base = g["s_h"].n

            def v_h(v, B0=B0, ft=ft, fft_base=fft_base, sg_now=sg_now, fslot=fslot,
                    shared=shared, hgate=hgate, wbe_tgt=wbe_tgt):
                v.wait_ge(g["s_fft"].h, fft_base + 2)
                v.wait_ge(g["s_sg"].h, sg_now)
                if ft == 0:
                    if hgate > 0:
                        v.wait_ge(g["s_z"].h, hgate)
                    if not shared:
                        v.wait_ge(g["s_wbe"].h, wbe_tgt)
                for th in range(NTH):
                    hb = arena[:, OFF_HB + ft * T + th * TH: OFF_HB + ft * T + (th + 1) * TH]
                    if shared:
                        v.tensor_tensor(hb, sgscr[:, (fslot * 2 + th) * TH:(fslot * 2 + th + 1) * TH],
                                        BANK(B0 + 2 + th), ALU.mult).then_inc(g["s_h"].h, 1)
                    else:
                        v.drain()
                        v.tensor_tensor(uwscr[:, :], BANK(B0 + 2 + th),
                                        wbscr[:, th * 512:(th + 1) * 512], ALU.mult)
                        v.drain()
                        v.tensor_tensor(hb, sgscr[:, (fslot * 2 + th) * TH:(fslot * 2 + th + 1) * TH],
                                        uwscr[:, :], ALU.mult).then_inc(g["s_h"].h, 1)
            PV.append(v_h)
            g["s_h"].bump(NTH)
            set_bank(B0, g["s_sg"], sg_base + 1)
            set_bank(B0 + 1, g["s_sg"], sg_base + 2)
            set_bank(B0 + 2, g["s_h"], h_base + 1)
            set_bank(B0 + 3, g["s_h"], h_base + 2)

        h_all = g["s_h"].n
        for dt_i in range(16):
            if shared:
                chunks = [(load_wchunk(dsrc(dt_i, 0), 1408), 0, 11),
                          (load_wchunk(dsrc(dt_i, 1), 1408), 11, 22)]
            else:
                chunks = [(load_wchunk(dsrc(dt_i, 0), 1408), 0, 11)]
            zb = [(zctr[0] + k) % 8 for k in range(NTH)]
            zctr[0] += NTH
            z_base = g["s_z"].n
            bwz = [bank_wait_t(zb[th]) for th in range(NTH)]

            def t_dn(t, chunks=chunks, dt_i=dt_i, h_all=h_all, zb=zb, bwz=bwz):
                for ch, _, _ in chunks:
                    t.wait_ge(ch[1], ch[2])
                if dt_i == 0:
                    t.wait_ge(g["s_h"].h, h_all)
                for th in range(NTH):
                    for (sh, n) in bwz[th]:
                        t.wait_ge(sh, n)
                    first = True
                    inst = None
                    for ci, (ch, fc0, fc1) in enumerate(chunks):
                        for fc in range(fc0, fc1):
                            inst = t.matmul(BANK(zb[th]),
                                            wap(ch, (fc - fc0) * 128, (fc - fc0 + 1) * 128),
                                            arena[:, OFF_HB + fc * T + th * TH: OFF_HB + fc * T + (th + 1) * TH],
                                            start=first, stop=(ci == len(chunks) - 1 and fc == fc1 - 1))
                            first = False
                    inst.then_inc(g["s_z"].h, 1)
            PT.append(t_dn)
            g["s_z"].bump(NTH)
            for ch, _, _ in chunks:
                chunk_free_at(ch, g["s_z"], z_base + NTH)

            def v_dn(v, dt_i=dt_i, z_base=z_base, zb=zb):
                for th in range(NTH):
                    v.wait_ge(g["s_z"].h, z_base + th + 1)
                    xs = x[:, dt_i * T + th * TH: dt_i * T + (th + 1) * TH]
                    v.scalar_tensor_tensor(xs, BANK(zb[th]), 0.5, xs, ALU.mult, ALU.add).then_inc(g["s_zf"].h, 1)
            PV.append(v_dn)
            for th in range(NTH):
                g["s_zf"].bump()
                set_bank(zb[th], g["s_zf"], g["s_zf"].n)
            zf_by_dt[dt_i] = g["s_zf"].n


    def ph_qkv(lnx_all):
        LNX = lambda dc, a, b: arena[:, OFF_LNX + dc * T + a: OFF_LNX + dc * T + b]

        knorm_pending = [None]

        def flush_knorm():
            if knorm_pending[0] is not None:
                tf, vf = knorm_pending[0]
                PT.append(tf)
                PV.append(vf)
                knorm_pending[0] = None

        for ft in range(16):
            ck = load_wchunk(wk_s[ft, :, :], 2048)
            B0 = (ft % 2) * 2
            kp_base = g["s_kproj"].n
            ke_base = g["s_ke"].n
            ks_need = (s_ks[ft % 2].h, s_ks[ft % 2].n) if ft >= 2 else None
            bwk = bank_wait_t(B0) + bank_wait_t(B0 + 1)

            def t_k(t, ck=ck, B0=B0, ft=ft, lnx_all=lnx_all, bwk=bwk):
                t.wait_ge(ck[1], ck[2])
                for (sh, n) in bwk:
                    t.wait_ge(sh, n)
                inst = None
                for th in range(NTH):
                    if ft == 0:
                        t.wait_ge(g["s_lnx"].h, lnx_all[th])
                    for dc in range(NDC):
                        inst = t.matmul(BANK(B0 + th), wap(ck, dc * 128, (dc + 1) * 128),
                                        LNX(dc, th * TH, (th + 1) * TH),
                                        start=(dc == 0), stop=(dc == NDC - 1))
                inst.then_inc(g["s_kproj"].h, 2)
            PT.append(t_k)
            g["s_kproj"].bump(2)
            chunk_free_at(ck, g["s_kproj"], kp_base + 2)

            def v_k(v, B0=B0, ft=ft, kp_base=kp_base, ks_need=ks_need):
                v.wait_ge(g["s_kproj"].h, kp_base + 2)
                if ks_need is not None:
                    v.wait_ge(ks_need[0], ks_need[1])
                for th in range(NTH):
                    v.tensor_copy(arena[:, OFF_KST + (ft % 2) * 1024 + th * TH: OFF_KST + (ft % 2) * 1024 + (th + 1) * TH],
                                  BANK(B0 + th)).then_inc(g["s_ke"].h, 1)
            PV.append(v_k)
            g["s_ke"].bump(2)
            set_bank(B0, g["s_ke"], ke_base + 1)
            set_bank(B0 + 1, g["s_ke"], ke_base + 2)
            ke_tgt = g["s_ke"].n
            kn1_base = g["s_kn1"].n
            kn2_base = g["s_kn2"].n

            def s_knsq(sc, ft=ft, ke_tgt=ke_tgt, kn2_base=kn2_base):
                sc.wait_ge(g["s_ke"].h, ke_tgt)
                if kn2_base > 0:
                    sc.wait_ge(g["s_kn2"].h, kn2_base)
                sc.activation(nscr[:, :],
                              arena[:, OFF_KST + (ft % 2) * 1024: OFF_KST + (ft % 2) * 1024 + 1024],
                              AF.Square).then_inc(g["s_kn1"].h, 1)
            PS.append(s_knsq)
            g["s_kn1"].bump()
            flush_knorm()
            bw45 = bank_wait_t(4) + bank_wait_t(5)
            kn1_tgt = g["s_kn1"].n

            def t_knmm(t, kn1_tgt=kn1_tgt, bw45=bw45):
                t.wait_ge(g["s_kn1"].h, kn1_tgt)
                for (sh, n) in bw45:
                    t.wait_ge(sh, n)
                t.matmul(psum[0:1, 4 * 512:4 * 512 + 512], onescol[:, :],
                         nscr[:, 0:512], start=True, stop=True)
                inst = t.matmul(psum[0:1, 5 * 512:5 * 512 + 512], onescol[:, :],
                                nscr[:, 512:1024], start=True, stop=True)
                inst.then_inc(g["s_kn2"].h, 1)
            g["s_kn2"].bump()
            kn2_tgt = g["s_kn2"].n

            def v_knred(v, ft=ft, kn2_tgt=kn2_tgt):
                v.wait_ge(g["s_kn2"].h, kn2_tgt)
                v.reduce_max(kq_sb[0:1, ft:ft + 1], psum[0:1, 4 * 512:4 * 512 + 1024], AX).then_inc(g["s_kn2"].h, 1)
            g["s_kn2"].bump()
            knorm_pending[0] = (t_knmm, v_knred)
            set_bank(4, g["s_kn2"], g["s_kn2"].n)
            set_bank(5, g["s_kn2"], g["s_kn2"].n)

            def y_k(y, ft=ft, ke_tgt=ke_tgt):
                y.wait_ge(g["s_ke"].h, ke_tgt)
                y.dma_start(out=k_send[ft * 131072: (ft + 1) * 131072].rearrange("(p t) -> p t", t=T),
                            in_=arena[:, OFF_KST + (ft % 2) * 1024: OFF_KST + (ft % 2) * 1024 + 1024]).then_inc(s_ks[ft % 2].h, 16)
            PY.append(y_k)
            s_ks[ft % 2].bump(16)
        flush_knorm()
        ksend_all = (s_ks[0].n, s_ks[1].n)

        kn_all = g["s_kn2"].n

        def y_kn(y, kn_all=kn_all):
            y.wait_ge(g["s_kn2"].h, kn_all)
            y.dma_start(out=k_send[2097152:2097168].rearrange("(a b) -> a b", a=1),
                        in_=kq_sb[0:1, :]).then_inc(g["s_knl"].h, 16)
        PY.append(y_kn)
        g["s_knl"].bump(16)
        knl_send = g["s_knl"].n

        def g_ccK(gp, ksend_all=ksend_all, knl_send=knl_send):
            gp.wait_ge(s_ks[0].h, ksend_all[0])
            gp.wait_ge(s_ks[1].h, ksend_all[1])
            gp.wait_ge(g["s_knl"].h, knl_send)
            gp.collective_compute(
                "AllGather", ALU.bypass,
                replica_groups=[[0, 1, 2, 3], [4, 5, 6, 7]],
                ins=[k_send], outs=[k_gath],
            ).then_inc(g["s_cc"].h, 1)
        PG.append(g_ccK)
        g["s_cc"].bump()

        for fcol in range(4):
            vp_base = g["s_vproj"].n
            chunks = [load_wchunk(wv_s[fcol, dcg, :, :], 2048) for dcg in range(3)]
            for dcg in range(3):
                chunk_free_at(chunks[dcg], g["s_vproj"], vp_base + dcg + 1)
            chunks.append(load_wchunk(wv_s[fcol, 3, :, :], 2048))
            bwv = [bank_wait_t(k) for k in range(8)]

            def t_v(t, chunks=chunks, bwv=bwv):
                inst = None
                for dcg in range(4):
                    ch = chunks[dcg]
                    t.wait_ge(ch[1], ch[2])
                    if dcg == 0:
                        for bw in bwv:
                            for (sh, n) in bw:
                                t.wait_ge(sh, n)
                    for dci in range(4):
                        dc = dcg * 4 + dci
                        for tk in range(8):
                            inst = t.matmul(BANK(tk), LNX(dc, tk * 128, (tk + 1) * 128),
                                            wap(ch, dci * 512, (dci + 1) * 512),
                                            start=(dc == 0), stop=(dc == NDC - 1))
                            if dc == NDC - 1:
                                inst.then_inc(g["s_vproj"].h, 1)
                            elif dci == 3 and tk == 7:
                                inst.then_inc(g["s_vproj"].h, 1)
            PT.append(t_v)
            g["s_vproj"].bump(11)
            chunk_free_at(chunks[3], g["s_vproj"], vp_base + 11)
            ve_base = g["s_ve"].n

            def v_v(v, fcol=fcol, vp_base=vp_base):
                v.wait_ge(g["s_vproj"].h, vp_base + 11)
                for tk in range(8):
                    inst = None
                    for hh in range(4):
                        h = fcol * 4 + hh
                        inst = v.tensor_copy(arena[:, OFF_VSB + h * 1024 + tk * 128: OFF_VSB + h * 1024 + (tk + 1) * 128],
                                             psum[:, tk * 512 + hh * 128: tk * 512 + (hh + 1) * 128])
                    inst.then_inc(g["s_ve"].h, 1)
            PV.append(v_v)
            g["s_ve"].bump(8)
            for k in range(8):
                set_bank(k, g["s_ve"], ve_base + k + 1)
        ve_all = g["s_ve"].n

        def y_v(y, ve_all=ve_all):
            y.wait_ge(g["s_ve"].h, ve_all)
            y.dma_start(out=v_send[:].rearrange("(p w) -> p w", w=16384),
                        in_=arena[:, OFF_VSB:OFF_VSB + 16384]).then_inc(g["s_vsend"].h, 16)
        PY.append(y_v)
        g["s_vsend"].bump(16)
        vsend_all = g["s_vsend"].n

        def g_ccV(gp, vsend_all=vsend_all):
            gp.wait_ge(g["s_vsend"].h, vsend_all)
            gp.collective_compute(
                "AllGather", ALU.bypass,
                replica_groups=[[0, 1, 2, 3], [4, 5, 6, 7]],
                ins=[v_send], outs=[v_gath],
            ).then_inc(g["s_cc"].h, 1)
        PG.append(g_ccV)
        g["s_cc"].bump()

        qnorm_pending = [None]

        def flush_qnorm():
            if qnorm_pending[0] is not None:
                tf, vf = qnorm_pending[0]
                PT.append(tf)
                PV.append(vf)
                qnorm_pending[0] = None

        for ft in range(16):
            cq = load_wchunk(wq_s[ft, :, :], 2048)
            B0 = (ft % 2) * 2
            qp_base = g["s_qproj"].n
            qe_base = g["s_qe"].n
            bwq = bank_wait_t(B0) + bank_wait_t(B0 + 1)

            def t_q(t, cq=cq, B0=B0, bwq=bwq):
                t.wait_ge(cq[1], cq[2])
                for (sh, n) in bwq:
                    t.wait_ge(sh, n)
                inst = None
                for dc in range(NDC):
                    for th in range(NTH):
                        inst = t.matmul(BANK(B0 + th), wap(cq, dc * 128, (dc + 1) * 128),
                                        LNX(dc, th * TH, (th + 1) * TH),
                                        start=(dc == 0), stop=(dc == NDC - 1))
                inst.then_inc(g["s_qproj"].h, 2)
            PT.append(t_q)
            g["s_qproj"].bump(2)
            chunk_free_at(cq, g["s_qproj"], qp_base + 2)

            def v_q(v, B0=B0, ft=ft, qp_base=qp_base):
                v.wait_ge(g["s_qproj"].h, qp_base + 2)
                for th in range(NTH):
                    v.tensor_copy(arena[:, OFF_QT + ft * T + th * TH: OFF_QT + ft * T + (th + 1) * TH],
                                  BANK(B0 + th)).then_inc(g["s_qe"].h, 1)
            PV.append(v_q)
            g["s_qe"].bump(2)
            set_bank(B0, g["s_qe"], qe_base + 1)
            set_bank(B0 + 1, g["s_qe"], qe_base + 2)
            qe_tgt2 = g["s_qe"].n
            qn1_base = g["s_qn1"].n
            qn2_base = g["s_qn2"].n

            def s_qnsq(sc, ft=ft, qe_tgt2=qe_tgt2, qn2_base=qn2_base):
                sc.wait_ge(g["s_qe"].h, qe_tgt2)
                if qn2_base > 0:
                    sc.wait_ge(g["s_qn2"].h, qn2_base)
                sc.activation(nscr[:, :],
                              arena[:, OFF_QT + ft * T: OFF_QT + ft * T + 1024],
                              AF.Square).then_inc(g["s_qn1"].h, 1)
            PS.append(s_qnsq)
            g["s_qn1"].bump()
            flush_qnorm()
            bw45q = bank_wait_t(4) + bank_wait_t(5)
            qn1_tgt = g["s_qn1"].n

            def t_qnmm(t, qn1_tgt=qn1_tgt, bw45q=bw45q):
                t.wait_ge(g["s_qn1"].h, qn1_tgt)
                for (sh, n) in bw45q:
                    t.wait_ge(sh, n)
                t.matmul(psum[0:1, 4 * 512:4 * 512 + 512], onescol[:, :],
                         nscr[:, 0:512], start=True, stop=True)
                inst = t.matmul(psum[0:1, 5 * 512:5 * 512 + 512], onescol[:, :],
                                nscr[:, 512:1024], start=True, stop=True)
                inst.then_inc(g["s_qn2"].h, 1)
            g["s_qn2"].bump()
            qn2_tgt = g["s_qn2"].n

            def v_qnred(v, ft=ft, qn2_tgt=qn2_tgt):
                v.wait_ge(g["s_qn2"].h, qn2_tgt)
                for qb in range(2):
                    v.reduce_max(qn_sb[0:1, ft * 2 + qb:ft * 2 + qb + 1],
                                 psum[0:1, (4 + qb) * 512:(4 + qb) * 512 + 512], AX).then_inc(g["s_qn2"].h, 1)
            g["s_qn2"].bump(2)
            qnorm_pending[0] = (t_qnmm, v_qnred)
            set_bank(4, g["s_qn2"], g["s_qn2"].n)
            set_bank(5, g["s_qn2"], g["s_qn2"].n)
        flush_qnorm()
        return g["s_qe"].n

    def ph_attn(qe_all):
        cc_tgt = g["s_cc"].n
        ccK_tgt = cc_tgt - 1
        vsend_all_gate = g["s_vsend"].n
        qe_gate = qe_all
        kslot_free = {0: None, 1: None, 2: None}
        vslot_free = {0: None, 1: None}
        first_load = [True]
        first_vload = [True]
        first_c = [True]
        vsb_view = v_send[:].rearrange("(p h k d) -> p h k d", h=16, k=8, d=128)
        KSL = 4096

        knl_pre = g["s_knl"].n

        def y_knb(y, knl_pre=knl_pre):
            y.wait_ge(g["s_cc"].h, ccK_tgt)
            y.dma_start(out=knb_sb[0:1, :].rearrange("a (r c) -> a r c", r=4),
                        in_=k_gath[:, 2097152:2097168].rearrange("r (a c) -> a r c", a=1)
                        ).then_inc(g["s_knl"].h, 16)
        PY.append(y_knb)
        g["s_knl"].bump(16)
        knl_tgt = g["s_knl"].n

        def emit_kload(h, qb):
            slot = h % 3
            sl = s_kl[slot]
            base = OFF_KB + slot * KSL
            need = kslot_free[slot]
            isfirst = first_load[0]
            first_load[0] = False

            def y_op(y, h=h, qb=qb, base=base, sl=sl, need=need, isfirst=isfirst):
                if isfirst:
                    y.wait_ge(g["s_cc"].h, ccK_tgt)
                    y.wait_ge(g["s_vsend"].h, vsend_all_gate)
                if need is not None:
                    y.wait_ge(need[0], need[1])
                y.dma_start(out=arena[:, base:base + 512],
                            in_=k_send[h * 131072: (h + 1) * 131072]
                            .rearrange("(p t) -> p t", t=T)[:, qb * 512:(qb + 1) * 512]).then_inc(sl.h, 16)
                gk = k_gath[:, h * 131072: (h + 1) * 131072].rearrange(
                    "r (p t) -> p r t", t=T)
                if qb == 0:
                    y.dma_start(out=arena[:, base + 512:base + 2048].rearrange("p (r t) -> p r t", t=512),
                                in_=gk[:, 0:3, 0:512]).then_inc(sl.h, 16)
                else:
                    y.dma_start(out=arena[:, base + 512:base + 2560].rearrange("p (r t) -> p r t", t=512),
                                in_=gk[:, :, 0:512]).then_inc(sl.h, 16)
                    y.dma_start(out=arena[:, base + 2560:base + 4096].rearrange("p (r t) -> p r t", t=512),
                                in_=gk[:, 1:4, 512:1024]).then_inc(sl.h, 16)
            PY.append(y_op)
            sl.bump(16 * (2 if qb == 0 else 3))
            return slot, sl.h, sl.n

        def emit_vload(h, qb):
            slot = h % 2
            sl = s_vl[slot]
            base = OFF_VB + slot * KSL
            need = vslot_free[slot]
            isfirst = first_vload[0]
            first_vload[0] = False

            def y_op(y, h=h, qb=qb, base=base, sl=sl, need=need, isfirst=isfirst):
                if isfirst:
                    y.wait_ge(g["s_qe"].h, qe_gate)
                    y.wait_ge(g["s_cc"].h, cc_tgt)
                if need is not None:
                    y.wait_ge(need[0], need[1])
                y.dma_start(out=arena[:, base:base + 512].rearrange("p (k d) -> p k d", d=128),
                            in_=vsb_view[:, h, qb * 4:(qb + 1) * 4, :]).then_inc(sl.h, 16)
                gv = v_gath[:, :].rearrange("r (p h k d) -> p r h k d", h=16, k=8, d=128)
                if qb == 0:
                    y.dma_start(out=arena[:, base + 512:base + 2048].rearrange("p (r k d) -> p r k d", k=4, d=128),
                                in_=gv[:, 0:3, h, 0:4, :]).then_inc(sl.h, 16)
                else:
                    y.dma_start(out=arena[:, base + 512:base + 2560].rearrange("p (r k d) -> p r k d", k=4, d=128),
                                in_=gv[:, :, h, 0:4, :]).then_inc(sl.h, 16)
                    y.dma_start(out=arena[:, base + 2560:base + 4096].rearrange("p (r k d) -> p r k d", k=4, d=128),
                                in_=gv[:, 1:4, h, 4:8, :]).then_inc(sl.h, 16)
            PY.append(y_op)
            sl.bump(16 * (2 if qb == 0 else 3))
            return slot, sl.h, sl.n

        def emit_c(h, qb):
            cslot = h % 2
            exp_now = g["s_exp"].n
            isfirst = first_c[0]
            first_c[0] = False

            def v_c(v, h=h, qb=qb, cslot=cslot, isfirst=isfirst):
                if isfirst:
                    v.wait_ge(g["s_knl"].h, knl_tgt)
                v.reduce_max(kmx[0:1, cslot:cslot + 1],
                             knb_sb[0:1, :].rearrange("a (r c) -> a r c", r=4)[:, :, h], AX)
                v.drain()
                v.tensor_tensor(cqs[0:1, cslot:cslot + 1], kmx[0:1, cslot:cslot + 1],
                                qn_sb[0:1, h * 2 + qb:h * 2 + qb + 1], ALU.mult).then_inc(g["s_cq"].h, 1)
            PV.append(v_c)
            g["s_cq"].bump()
            cq_tgt = g["s_cq"].n

            def s_c(sc, cslot=cslot, cq_tgt=cq_tgt):
                sc.wait_ge(g["s_cq"].h, cq_tgt)
                sc.activation(cscs[0:1, cslot:cslot + 1], cqs[0:1, cslot:cslot + 1],
                              AF.Sqrt, scale=1.21).then_inc(g["s_csc"].h, 1)
            PS.append(s_c)
            g["s_csc"].bump()
            csc_tgt = g["s_csc"].n
            ncb_pre = g["s_sa"].n

            def v_crow(v, cslot=cslot, csc_tgt=csc_tgt, exp_now=exp_now, ncb_pre=ncb_pre):
                v.wait_ge(g["s_csc"].h, csc_tgt)
                if ncb_pre > 0:
                    v.wait_ge(g["s_sa"].h, ncb_pre)
                v.tensor_scalar(crow[0:1, :], negisq[:, :],
                                cscs[0:1, cslot:cslot + 1], None, ALU.mult).then_inc(g["s_cq"].h, 1)
            PV.append(v_crow)
            g["s_cq"].bump()
            crow_tgt = g["s_cq"].n

            def y_ncb(y, cslot=cslot, crow_tgt=crow_tgt, exp_now=exp_now):
                y.wait_ge(g["s_cq"].h, crow_tgt)
                y.wait_ge(g["s_exp"].h, exp_now)
                y.dma_start(out=negc[:, cslot:cslot + 1],
                            in_=crow[0:1, :]).then_inc(g["s_sa"].h, 16)
            PY.append(y_ncb)
            g["s_sa"].bump(16)
            ncb_tgt = g["s_sa"].n

            def v_c3(v, ncb_tgt=ncb_tgt, cslot=cslot, qb=qb, exp_now=exp_now):
                v.wait_ge(g["s_sa"].h, ncb_tgt)
                v.wait_ge(g["s_exp"].h, exp_now)
                ncols = 3 if qb == 0 else 7
                pb = padbias[:, 0:3] if qb == 0 else padbias[:, 3:10]
                v.tensor_scalar(btab[:, cslot * 8:cslot * 8 + ncols], pb, negc[:, cslot:cslot + 1],
                                None, ALU.add).then_inc(g["s_cb"].h, 1)
            PV.append(v_c3)
            g["s_cb"].bump()
            return cslot, g["s_cb"].n

        def emit_B(h, qb, kinfo, vinfo, cinfo):
            kslot, ksemh, ktgt = kinfo
            vslot, vsemh, vtgt = vinfo
            cslot, cb_tgt = cinfo
            kbase = OFF_KB + kslot * KSL
            vbase = OFF_VB + vslot * KSL
            nkb = 4 if qb == 0 else 8
            nkt = nkb * 4
            npair = nkt // 2
            qoff = OFF_QT + h * T + qb * TH
            zbank = 4 + (h % 2)
            sb_base = g["s_sb"].n
            exp_base = g["s_exp"].n
            pf_base = g["s_pf"].n
            inv_prev = g["s_inv"].n
            bwz = bank_wait_t(zbank)
            bw0123 = bank_wait_t(0) + bank_wait_t(1) + bank_wait_t(2) + bank_wait_t(3)

            def t_op(t, h=h, qb=qb, kbase=kbase, vbase=vbase, ksemh=ksemh, ktgt=ktgt,
                     vsemh=vsemh, vtgt=vtgt, cb_tgt=cb_tgt, nkt=nkt, npair=npair,
                     qoff=qoff, zbank=zbank, exp_base=exp_base, inv_prev=inv_prev,
                     bwz=bwz, bw0123=bw0123):
                t.wait_ge(ksemh, ktgt)
                t.wait_ge(vsemh, vtgt)
                t.wait_ge(g["s_cb"].h, cb_tgt)
                for (sh, n) in bwz + bw0123:
                    t.wait_ge(sh, n)
                t.wait_ge(g["s_inv"].h, inv_prev)

                def s_mm(kt):
                    sbk = kt % 4
                    pr = kt // 2
                    if pr >= 2:
                        t.wait_ge(g["s_exp"].h, exp_base + pr - 1)
                    inst = t.matmul(BANK(sbk), arena[:, kbase + kt * 128: kbase + (kt + 1) * 128],
                                    arena[:, qoff:qoff + TH],
                                    start=True, stop=(kt >= 4))
                    if kt < 4:
                        inst = t.matmul(BANK(sbk), identb[:, :],
                                        m0[:, 384 - kt * 128:384 - kt * 128 + 512],
                                        start=False, stop=True)
                    inst.then_inc(g["s_sb"].h, 1)

                def av_mm(kt):
                    t.wait_ge(g["s_exp"].h, exp_base + kt // 2 + 1)
                    pb = arena[:, OFF_PB + (kt % 4) * 512: OFF_PB + (kt % 4 + 1) * 512]
                    t.matmul(BANK(zbank), arena[:, vbase + kt * 128: vbase + (kt + 1) * 128], pb,
                             start=(kt == 0), stop=(kt == nkt - 1))
                    if kt < 4:
                        lcol = onescol[:, 0:1]
                    else:
                        ci = (kt // 4 - 1) + (0 if qb == 0 else 3)
                        lcol = sumind[:, ci:ci + 1]
                    inst = t.matmul(psum[0:1, 6 * 512:6 * 512 + 512], lcol, pb,
                                    start=(kt == 0), stop=(kt == nkt - 1))
                    inst.then_inc(g["s_pf"].h, 1)

                for pr in range(npair):
                    s_mm(2 * pr)
                    s_mm(2 * pr + 1)
                    if pr >= 1:
                        av_mm(2 * pr - 2)
                        av_mm(2 * pr - 1)
                av_mm(nkt - 2)
                av_mm(nkt - 1)
            PT.append(t_op)
            g["s_sb"].bump(nkt)
            g["s_pf"].bump(nkt)

            def s_op(sc, qb=qb, npair=npair, sb_base=sb_base, pf_base=pf_base, cslot=cslot,
                     cb_tgt=cb_tgt):
                for pr in range(npair):
                    sc.wait_ge(g["s_sb"].h, sb_base + 2 * pr + 2)
                    if pr == 0:
                        sc.wait_ge(g["s_cb"].h, cb_tgt)
                    if pr >= 2:
                        sc.wait_ge(g["s_pf"].h, pf_base + 2 * pr - 2)
                    sbk = (2 * pr) % 4
                    pb = arena[:, OFF_PB + ((2 * pr) % 4) * 512: OFF_PB + ((2 * pr) % 4) * 512 + 1024]
                    blk = (2 * pr) // 4
                    if blk == 0:
                        inst = sc.activation(pb, psum[:, sbk * 512: sbk * 512 + 1024], AF.Exp,
                                             bias=negc[:, cslot:cslot + 1], scale=ISQ)
                    else:
                        ci = blk - 1
                        col = cslot * 8 + ci
                        scol = ci if qb == 0 else 3 + ci
                        inst = sc.activation(pb, psum[:, sbk * 512: sbk * 512 + 1024], AF.Exp,
                                             bias=btab[:, col:col + 1],
                                             scale=scaletab[:, scol:scol + 1])
                    inst.then_inc(g["s_exp"].h, 1)
            PS.append(s_op)
            g["s_exp"].bump(npair)

            kslot_free[kslot] = (g["s_sb"].h, g["s_sb"].n)
            vslot_free[vslot] = (g["s_pf"].h, g["s_pf"].n)

            pf_all = g["s_pf"].n
            islot = h % 2

            def v_inv(v, pf_all=pf_all, islot=islot):
                v.wait_ge(g["s_pf"].h, pf_all)
                v.reciprocal(invsc[0:1, islot * 512:(islot + 1) * 512], psum[0:1, 6 * 512:6 * 512 + 512]).then_inc(g["s_inv"].h, 1)
            PV.append(v_inv)
            g["s_inv"].bump()
            for k in range(4):
                set_bank(k, g["s_exp"], g["s_exp"].n)
            return (h, islot, zbank, g["s_inv"].n)

        def emit_B_tail(binfo):
            h, islot, zbank, inv_tgt = binfo
            zev_prev = g["s_zev"].n

            def t_invb(t, inv_tgt=inv_tgt, islot=islot, zev_prev=zev_prev):
                t.wait_ge(g["s_inv"].h, inv_tgt)
                t.wait_ge(g["s_zev"].h, zev_prev)
                inst = t.matmul(BANK(7), onesrow[:, :], invsc[0:1, islot * 512:(islot + 1) * 512],
                                start=True, stop=True)
                inst.then_inc(g["s_invb"].h, 1)
            PT.append(t_invb)
            g["s_invb"].bump()
            invb_tgt = g["s_invb"].n
            zgate = zgate_holder[0]
            zgate_holder[0] = None

            def v_z(v, h=h, invb_tgt=invb_tgt, zbank=zbank, zgate=zgate):
                v.wait_ge(g["s_invb"].h, invb_tgt)
                if zgate is not None:
                    v.wait_ge(zgate[0], zgate[1])
                v.tensor_copy(uwscr[:, 0:512], BANK(7))
                v.drain()
                v.tensor_tensor(arena[:, OFF_Z + h * 512: OFF_Z + (h + 1) * 512],
                                BANK(zbank), uwscr[:, 0:512], ALU.mult).then_inc(g["s_zev"].h, 1)
            PV.append(v_z)
            g["s_zev"].bump()
            set_bank(zbank, g["s_zev"], g["s_zev"].n)

        zgate_holder = [None]

        def emit_oproj(qb):
            zev_all = g["s_zev"].n
            for dt_i in range(16):
                co = load_wchunk(wo_s[dt_i, :, :], 2048)
                ob = dt_i % 4
                o_base = g["s_o"].n
                bwo = bank_wait_t(ob)

                def t_o(t, co=co, dt_i=dt_i, ob=ob, zev_all=zev_all, bwo=bwo):
                    t.wait_ge(co[1], co[2])
                    if dt_i == 0:
                        t.wait_ge(g["s_zev"].h, zev_all)
                    for (sh, n) in bwo:
                        t.wait_ge(sh, n)
                    inst = None
                    for hc in range(16):
                        inst = t.matmul(BANK(ob), wap(co, hc * 128, (hc + 1) * 128),
                                        arena[:, OFF_Z + hc * 512: OFF_Z + (hc + 1) * 512],
                                        start=(hc == 0), stop=(hc == 15))
                    inst.then_inc(g["s_o"].h, 1)
                PT.append(t_o)
                g["s_o"].bump()
                chunk_free_at(co, g["s_o"], o_base + 1)

                def v_o(v, dt_i=dt_i, ob=ob, o_base=o_base, qb=qb):
                    v.wait_ge(g["s_o"].h, o_base + 1)
                    xs = x[:, dt_i * T + qb * TH: dt_i * T + qb * TH + TH]
                    v.tensor_tensor(xs, xs, BANK(ob), ALU.add).then_inc(g["s_of"].h, 1)
                PV.append(v_o)
                g["s_of"].bump()
                set_bank(ob, g["s_of"], g["s_of"].n)
                of_by_dt[dt_i] = g["s_of"].n

        for qb in (0, 1):
            if qb == 1:
                zgate_holder[0] = (g["s_o"].h, g["s_o"].n)
            kinfos, vinfos, cinfos, binfos = {}, {}, {}, {}
            for i in range(18):
                if i < 16:
                    kinfos[i] = emit_kload(i, qb)
                    vinfos[i] = emit_vload(i, qb)
                    cinfos[i] = emit_c(i, qb)
                if 1 <= i <= 16:
                    hh = i - 1
                    binfos[hh] = emit_B(hh, qb, kinfos[hh], vinfos[hh], cinfos[hh])
                if 2 <= i <= 17:
                    emit_B_tail(binfos[i - 2])
            emit_oproj(qb)
        set_bank(6, g["s_inv"], g["s_inv"].n)
        set_bank(7, g["s_zev"], g["s_zev"].n)

    lnx1 = ph_ln([(g["s_x"].h, g["s_x"].n), (g["s_const"].h, g["s_const"].n)])
    ph_router(lnx1)
    for e_idx in range(9):
        ph_expert(e_idx, 0)

    if ATTN_ON:
        lnx2 = ph_ln([], gate_by_dc=(g["s_zf"].h, dict(zf_by_dt)))
        qe_all = ph_qkv(lnx2)
        ph_attn(qe_all)

    if FFN2_ON:
        if ATTN_ON:
            lnx3 = ph_ln([], gate_by_dc=(g["s_of"].h, dict(of_by_dt)))
        else:
            lnx3 = ph_ln([(g["s_zf"].h, g["s_zf"].n)])
        ph_router(lnx3)
        for e_idx in range(9):
            ph_expert(e_idx, 1)

    zf_total = g["s_zf"].n
    of_total = g["s_of"].n

    out_tgt = g["s_out"].n + 16 * NDC

    def y_final(y, zf_total=zf_total, of_total=of_total, out_tgt=out_tgt):
        y.wait_ge(g["s_zf"].h, zf_total)
        if ATTN_ON and not FFN2_ON:
            y.wait_ge(g["s_of"].h, of_total)
        for dc in range(NDC):
            y.dma_start(out=y_out[dc, :, :], in_=x[:, dc * T:(dc + 1) * T]).then_inc(g["s_out"].h, 16)
        y.wait_ge(g["s_out"].h, out_tgt)
    PY.append(y_final)
    g["s_out"].bump(16 * NDC)

    with nc.Block() as block:
        @block.tensor
        def _(t):
            for f in PT:
                f(t)

        @block.vector
        def _(v):
            for f in PV:
                f(v)

        @block.scalar
        def _(sc):
            for f in PS:
                f(sc)

        @block.sync
        def _(y):
            for f in PY:
                f(y)

        @block.gpsimd
        def _(gp):
            for f in PG:
                f(gp)

    stack.close()
    return nc


def up_stream(WT, nft):
    A = WT.reshape(16, 128, nft, 128)
    return np.ascontiguousarray(A.transpose(2, 1, 0, 3).reshape(nft, 128, 2048))


def down_stream(WdT):
    nfc = WdT.shape[0] // 128
    A = WdT.reshape(nfc, 128, 16, 128)
    return np.ascontiguousarray(A.transpose(2, 1, 0, 3).reshape(16, 128, nfc * 128))


def wv_stream(WvT):
    A = WvT.reshape(4, 4, 128, 4, 512)
    return np.ascontiguousarray(A.transpose(3, 0, 2, 1, 4).reshape(4, 4, 128, 2048))


def host_prepare(inputs):
    f32 = np.float32
    gw = lambda k: np.asarray(inputs[k], f32)
    Wq, Wk, Wv, Wo = gw("L0_attn_Wq"), gw("L0_attn_Wk"), gw("L0_attn_Wv"), gw("L0_attn_Wo")
    Wr, Wg, Wu, Wd = gw("L0_ffn_Wr"), gw("L0_ffn_Wg"), gw("L0_ffn_Wu"), gw("L0_ffn_Wd")
    Wsg, Wsu, Wsd = gw("L0_ffn_Wsg"), gw("L0_ffn_Wsu"), gw("L0_ffn_Wsd")
    x = gw("L0_attn_x")

    shared_w = {
        "wr_in": np.ascontiguousarray(Wr.T.reshape(16, 128, 8).transpose(1, 0, 2).reshape(128, 128)),
        "wsg_s": up_stream(np.ascontiguousarray(Wsg.T), NFT_S),
        "wsu_s": up_stream(np.ascontiguousarray(Wsu.T), NFT_S),
        "wg_s": np.stack([up_stream(np.ascontiguousarray(Wg[e].T), NFT_E) for e in range(NE)]),
        "wu_s": np.stack([up_stream(np.ascontiguousarray(Wu[e].T), NFT_E) for e in range(NE)]),
        "wsd_s": down_stream(np.ascontiguousarray(Wsd.T)),
        "wd_s": np.stack([down_stream(np.ascontiguousarray(Wd[e].T)) for e in range(NE)]),
        "wk_s": up_stream(np.ascontiguousarray(Wk.T), 16),
        "wv_s": wv_stream(np.ascontiguousarray(Wv.T)),
        "wq_s": up_stream(np.ascontiguousarray(Wq.T), 16),
        "wo_s": up_stream(np.ascontiguousarray(Wo.T), 16),
    }
    p_idx = np.arange(128)[:, None]
    s_idx = np.arange(896)[None, :]
    consts = {
        "m0_in": np.where(s_idx < p_idx + 384, BIGNEG, 0.0).astype(BF),
        "identf_in": np.eye(128, dtype=f32),
        "identb_in": np.eye(128).astype(BF),
        "c2048_in": np.full((128, 1), 1.0 / 2048.0).astype(BF),
        "onescol_in": np.ones((128, 1), BF),
        "onesrow_in": np.ones((1, 128), f32),
        "negisq_in": np.full((1, 128), -ISQ, f32),
        "fconst_in": np.ascontiguousarray(
            np.stack([np.full(128, EPS, f32), np.full(128, ISQ, f32),
                      np.full(128, 1.21, f32)], axis=1)),
    }
    in_maps = []
    for c in range(8):
        b, j = c // 4, c % 4
        blkA, blkB = j, 7 - j
        xl = np.concatenate([x[b, blkA * 512:(blkA + 1) * 512],
                             x[b, blkB * 512:(blkB + 1) * 512]], axis=0)
        x_fm = np.ascontiguousarray(xl.T.reshape(16, 128, T))
        slots_a = [0, 1, 2]
        slots_b = [0, 1, 2, 3, 6, 5, 4]
        cnt = np.asarray([1.0 if gg < blkA else 0.0 for gg in slots_a] +
                         [1.0 if gg < blkB else 0.0 for gg in slots_b], f32)
        im = {
            "x_in": x_fm,
            "sumind_in": np.ascontiguousarray(np.broadcast_to(cnt, (128, 10))).astype(BF),
            "scaletab_in": np.ascontiguousarray(np.broadcast_to(np.where(cnt > 0, ISQ, 0.0).astype(f32), (128, 10))),
            "padbias_in": np.ascontiguousarray(np.broadcast_to(np.where(cnt > 0, 0.0, PADB).astype(f32), (128, 10))),
        }
        im.update(shared_w)
        im.update(consts)
        in_maps.append(im)
    return in_maps


def host_gather(results):
    out = np.empty((2, 4096, 2048), np.float32)
    for c in range(8):
        b, j = c // 4, c % 4
        y = np.asarray(results[c]["y_out"], np.float32)
        xl = y.reshape(2048, T).T
        out[b, j * 512:(j + 1) * 512] = xl[0:512]
        out[b, (7 - j) * 512:(8 - j) * 512] = xl[512:1024]
    return out


def kernel(**inputs):
    in_maps = host_prepare(inputs)
    nc = build_program()
    res = run_bass_kernel_spmd(nc, in_maps, core_ids=list(range(8)))
    return host_gather(res.results)
